# version 10
# baseline (speedup 1.0000x reference)
import numpy as np
import concourse.bass as bass
import concourse.mybir as mybir
from concourse.bass_utils import run_bass_kernel_spmd
from concourse.alu_op_type import AluOpType as OP

F32 = mybir.dt.float32
I32 = mybir.dt.int32
U32 = mybir.dt.uint32
AF = mybir.ActivationFunctionType
AXX = mybir.AxisListType.X

NCORE = 8
T = 2048
D = 1024
KD = 8
FF = 4096
KF = 32
NE = 8
FI = 2816
KI = 22
NST = 16
TC = 256
NCH = T // TC
LM = 128
TBK = 512
EPS = 1e-5
RK = 640.0
R127 = 1.0 / 127.0

ENGS = ("sync", "tensor", "vector", "scalar", "gpsimd")
DEBUG = False
STOP_AFTER = None

_off = 0
def _nxt(n):
    global _off
    o = _off
    _off += n
    return o
CO = {}
CO["ones"] = _nxt(1)
CO["dec"] = _nxt(TC)
CO["ln1g"] = _nxt(KD)
CO["ln1b"] = _nxt(KD)
CO["ln2g"] = _nxt(KD)
CO["ln2b"] = _nxt(KD)
CO["b1f"] = _nxt(KF)
CO["hb1"] = _nxt(KF)
CO["b2f"] = _nxt(KD)
CO["hb2"] = _nxt(KD)
CO["moeb1"] = _nxt(NE * KF)
CO["bin"] = _nxt(16)
CO["convw"] = _nxt(KD * 4)
CO["convb"] = _nxt(KD)
CO["bdt"] = _nxt(KD)
CO["Aneg"] = _nxt(KD * NST)
CO["Dv"] = _nxt(KD)
CO["bout"] = _nxt(KD)
CO["lngm"] = _nxt(KD)
CO["lnbm"] = _nxt(KD)
CO["wbc"] = _nxt(KD * 32)
CO["rwt"] = _nxt(KD * NE)
CO["bbc"] = _nxt(1)
CV_W = _off + (0 if _off % 2 == 0 else 1)


class TB:
    __slots__ = ("h", "name", "fsz")

    def __init__(self, h, name, fsz):
        self.h = h
        self.name = name
        self.fsz = fsz

    def ap(self):
        return self.h.ap()

    def sl(self, p0, np_, off, dims):
        return bass.AP(tensor=self.h, offset=p0 * self.fsz + off,
                       ap=[[self.fsz, np_]] + [list(d) for d in dims])

    def c(self, off, n, p0=0, np_=128):
        return self.sl(p0, np_, off, [[1, n]])


class Prog:
    def __init__(self):
        self.nc = bass.Bass("TRN2", target_bir_lowering=False, debug=False)
        self.q = {e: [] for e in ENGS}
        self.cnt = {e: 0 for e in ENGS}
        self.ndma = {e: 0 for e in ENGS}
        self.dcnt = {(e, j): 0 for e in ENGS for j in range(8)}
        self.seen = {e: {} for e in ENGS}
        self.lastw = {}
        self.readers = {}
        self.nops = 0

    def _need(self, e, key, val):
        if key[0] == "dma":
            val = self.dcnt[(key[1], key[2])]
        if self.seen[e].get(key, 0) >= val:
            return
        self.q[e].append(("w", key, val))
        self.seen[e][key] = val

    def _dep(self, e, reads, writes):
        for r in reads:
            lw = self.lastw.get(id(r))
            if lw is not None:
                self._need(e, lw[0], lw[1])
        for w in writes:
            lw = self.lastw.get(id(w))
            if lw is not None:
                self._need(e, lw[0], lw[1])
            for key, val in self.readers.get(id(w), {}).items():
                self._need(e, key, val)

    def _mark(self, key, val, reads, writes):
        for w in writes:
            self.lastw[id(w)] = (key, val)
            self.readers[id(w)] = {}
        for r in reads:
            d = self.readers.setdefault(id(r), {})
            if d.get(key, 0) < val:
                d[key] = val

    def op(self, e, fn, reads=(), writes=()):
        self._dep(e, reads, writes)
        self.cnt[e] += 1
        key, val = ("ord", e), self.cnt[e]
        self.q[e].append(("i", fn, [(key, 1)]))
        self._mark(key, val, reads, writes)
        self.nops += 1

    def dma(self, e, out_ap, in_ap, reads=(), writes=(), accum=False):
        self._dep(e, reads, writes)
        j = self.ndma[e] % 8
        self.ndma[e] += 1
        self.dcnt[(e, j)] += 16
        key, val = ("dma", e, j), self.dcnt[(e, j)]
        if accum:
            def fn(eng, o=out_ap, i=in_ap):
                return eng.dma_start(out=o, in_=i, accum_op=OP.add)
        else:
            def fn(eng, o=out_ap, i=in_ap):
                return eng.dma_start(out=o, in_=i)
        self.q[e].append(("i", fn, [(key, 16)]))
        self._mark(key, val, reads, writes)
        self.nops += 1

    def idma(self, out_ap, out_off, in_ap, in_off, reads=(), writes=(), accum=False):
        e = "gpsimd"
        self._dep(e, reads, writes)
        j = self.ndma[e] % 8
        self.ndma[e] += 1
        self.dcnt[(e, j)] += 16
        key, val = ("dma", e, j), self.dcnt[(e, j)]
        op = OP.add if accum else OP.bypass
        def fn(eng, o=out_ap, oo=out_off, i=in_ap, io=in_off, cop=op):
            return eng.indirect_dma_start(out=o, out_offset=oo, in_=i,
                                          in_offset=io, compute_op=cop)
        self.q[e].append(("i", fn, [(key, 16)]))
        self._mark(key, val, reads, writes)
        self.nops += 1

    def barrier_wait(self, e, tbs):
        self._dep(e, tbs, ())

    def merge_alias(self, dst, srcs):
        for src in srcs:
            lw = self.lastw.get(id(src))
            if lw is not None:
                d = self.readers.setdefault(id(dst), {})
                if d.get(lw[0], 0) < lw[1]:
                    d[lw[0]] = lw[1]
            for key, val in self.readers.get(id(src), {}).items():
                d = self.readers.setdefault(id(dst), {})
                if d.get(key, 0) < val:
                    d[key] = val

    def finalize(self):
        import contextlib
        nc = self.nc
        with contextlib.ExitStack() as st:
            sems = {}
            for e in ENGS:
                sems[("ord", e)] = st.enter_context(nc.semaphore(f"o_{e}"))
                if self.ndma[e]:
                    for j in range(8):
                        sems[("dma", e, j)] = st.enter_context(
                            nc.semaphore(f"d_{e}{j}"))
            blk = st.enter_context(nc.Block())
            for e in ENGS:
                items = self.q[e]
                if not items:
                    continue

                def body(eng, items=items, sems=sems):
                    for it in items:
                        if it[0] == "w":
                            eng.wait_ge(sems[it[1]], it[2])
                        else:
                            inst = it[1](eng)
                            for key, d in it[2]:
                                inst.then_inc(sems[key], d)
                getattr(blk, e)(body)


def build_program():
    P = Prog()
    nc = P.nc

    def dram_in(name, shape):
        return TB(nc.dram_tensor(name, list(shape), F32, kind="ExternalInput"),
                  name, int(np.prod(shape[1:])))

    def dram_scr(name, shape):
        return TB(nc.dram_tensor(name, list(shape), F32, kind="Internal"),
                  name, int(np.prod(shape[1:])))

    def sbuf(name, fsz, dtype=F32):
        return TB(nc.alloc_sbuf_tensor(name, [128, fsz], dtype), name, fsz)

    def psum(name, fsz):
        return TB(nc.alloc_psum_tensor(name, [128, fsz], F32), name, fsz)

    x_in = dram_in("x_fm", (D, T))
    wrt = dram_in("wrt", (D, D))
    wkt = dram_in("wkt", (D, D))
    wvt = dram_in("wvt", (D, D))
    wot = dram_in("wot", (D, D))
    w1t = dram_in("w1t", (D, FF))
    w2t = dram_in("w2t", (FF, D))
    mw1t = dram_in("mw1t", (NE * D, FF))
    mw2t = dram_in("mw2t", (NE * FF, D))
    b2moe = dram_in("b2moe", (NE, D))
    w_in = dram_in("w_in", (D, 2 * D))
    w_dt = dram_in("w_dt", (D, D))
    w_out = dram_in("w_out", (D, D))
    wg_raw = dram_in("wg_raw", (FI, D))
    wu_raw = dram_in("wu_raw", (FI, D))
    wd_raw = dram_in("wd_raw", (D, FI))
    cvec = dram_in("cvec", (128, CV_W))
    clips = dram_in("clips", (1, 2))

    y_out = TB(nc.dram_tensor("y_fm", [D, T], F32, kind="ExternalOutput"),
               "y_fm", T)

    xn0_d = dram_scr("xn0_d", (D, T))
    xn1_d = dram_scr("xn1_d", (D, T))
    moe_d = dram_scr("moe_d", (T + 4, D))
    xg_d = dram_scr("xg_d", (D, T))
    dt_d = dram_scr("dt_d", (D, T))
    ys_d = dram_scr("ys_d", (D, T))
    bc_d = dram_scr("bc_d", (32, T))
    h_d = dram_scr("h_d", (FI, T))
    hq_d = dram_scr("hq_d", (FI, T))
    xn_tm = dram_scr("xn_tm", (T + 1, D))
    list_d = dram_scr("list_d", (NE, 2176, 2))
    cc_in1 = dram_scr("cc_in1", (1, 4))
    cc_out1 = dram_scr("cc_out1", (1, 4))
    cc_in2 = dram_scr("cc_in2", (1, 4))
    cc_out2 = dram_scr("cc_out2", (1, 4))

    X = sbuf("X", KD * T)
    A8a = sbuf("A8a", 8192)
    A8b = sbuf("A8b", 8192)
    XN2 = sbuf("XN2", 2048)
    GTMP = sbuf("GTMP", 256)
    A2 = [sbuf(f"A2_{i}", 2048) for i in range(4)]
    WR1 = sbuf("WR1", 2048)
    WR2 = sbuf("WR2", 2048)
    CV = sbuf("CV", CV_W)
    SM = sbuf("SM", 1024)
    SM2 = sbuf("SM2", 1024)
    MRE = sbuf("MRE", 512)
    RRE = sbuf("RRE", 512)
    STV = sbuf("STV", 128)
    MST = sbuf("MST", 128)
    ME = sbuf("ME", 128)
    LG = sbuf("LG", 128)
    IDXU = sbuf("IDXU", 8, U32)
    SID = sbuf("SID", 128)
    WBCs = sbuf("WBCs", 256)
    TRI = sbuf("TRI", 128)
    TOKI = sbuf("TOKI", 16)
    GSC = sbuf("GSC", 16)
    SXR = sbuf("SXR", 4)
    ITI = sbuf("ITI", 512, I32)

    PS0 = psum("PS0", 512)
    PS1 = psum("PS1", 512)
    PS2 = psum("PS2", 512)
    PS3 = psum("PS3", 512)
    PSW0 = psum("PSW0", 1024)
    PSW1 = psum("PSW1", 1024)

    def mm(pst, out_ap, lhsT_ap, rhs_ap, first, last, reads):
        def fn(eng, o=out_ap, l=lhsT_ap, r=rhs_ap, f=first, s=last):
            return eng.matmul(o, l, r, start=f, stop=s)
        P.op("tensor", fn, reads=reads, writes=[pst])

    def tpose(pst, out_ap, in_ap, reads):
        def fn(eng, o=out_ap, i=in_ap, idm=SID.c(0, 128)):
            return eng.transpose(o, i, idm)
        P.op("tensor", fn, reads=list(reads) + [SID], writes=[pst])

    def act(out_ap, in_ap, func, reads, writes, scale=1.0, bias=0.0):
        def fn(eng, o=out_ap, i=in_ap, f=func, s=scale, b=bias):
            return eng.activation(out=o, in_=i, func=f, bias=b, scale=s)
        P.op("scalar", fn, reads=reads, writes=writes)

    def v_tt(out_ap, a_ap, b_ap, opk, reads, writes):
        def fn(e, o=out_ap, a=a_ap, b=b_ap, k=opk):
            return e.tensor_tensor(out=o, in0=a, in1=b, op=k)
        P.op("vector", fn, reads=reads, writes=writes)

    def v_ts(out_ap, in_ap, s1, o1, s2, o2, reads, writes):
        def fn(e, o=out_ap, i=in_ap, a=s1, ka=o1, b=s2, kb=o2):
            return e.tensor_scalar(out=o, in0=i, scalar1=a, scalar2=b, op0=ka, op1=kb)
        P.op("vector", fn, reads=reads, writes=writes)

    def v_ts1(out_ap, in_ap, s1, o1, reads, writes):
        def fn(e, o=out_ap, i=in_ap, a=s1, ka=o1):
            return e.tensor_single_scalar(out=o, in_=i, scalar=a, op=ka)
        P.op("vector", fn, reads=reads, writes=writes)

    def v_stt(out_ap, in0_ap, scal, in1_ap, o0, o1, reads, writes):
        def fn(e, o=out_ap, i0=in0_ap, s=scal, i1=in1_ap, k0=o0, k1=o1):
            return e.scalar_tensor_tensor(out=o, in0=i0, scalar=s, in1=i1,
                                          op0=k0, op1=k1)
        P.op("vector", fn, reads=reads, writes=writes)

    def v_copy(out_ap, in_ap, reads, writes):
        def fn(e, o=out_ap, i=in_ap):
            return e.tensor_copy(out=o, in_=i)
        P.op("vector", fn, reads=reads, writes=writes)

    def v_red(out_ap, in_ap, opk, reads, writes, absv=False):
        def fn(e, o=out_ap, i=in_ap, k=opk, a=absv):
            return e.tensor_reduce(out=o, in_=i, axis=AXX, op=k,
                                   apply_absolute_value=a)
        P.op("vector", fn, reads=reads, writes=writes)

    def v_recip(out_ap, in_ap, reads, writes):
        def fn(e, o=out_ap, i=in_ap):
            return e.reciprocal(out=o, in_=i)
        P.op("vector", fn, reads=reads, writes=writes)

    def memset(tb, val, off=0, n=None, eng="vector"):
        n = tb.fsz if n is None else n
        def fn(e, a=tb.c(off, n), v=val):
            return e.memset(a, v)
        P.op(eng, fn, reads=(), writes=[tb])

    def load_group(ring, slot, src, rows0, nrows_k, col0, width=128):
        ap = bass.AP(tensor=src.h, offset=rows0 * src.fsz + col0,
                     ap=[[src.fsz, 128], [128 * src.fsz, nrows_k], [1, width]])
        P.dma("sync", ring.sl(0, 128, slot * 1024, [[width, nrows_k], [1, width]]),
              ap, reads=[src], writes=[ring])

    def rsl(ring, slot, j, width=128):
        return ring.sl(0, 128, slot * 1024 + j * width, [[1, width]])

    def fm_dram(dst, k, t0, n):
        return bass.AP(tensor=dst.h, offset=k * 128 * T + t0,
                       ap=[[T, 128], [1, n]])

    def ln_stats(chunk_ap_fn, tlen, reads_of):
        for k in range(KD):
            src_ap, src_tb = chunk_ap_fn(k)
            mm(PS2, PS2.c(0, tlen, 0, 1), CV.c(CO["ones"], 1), src_ap,
               k == 0, k == KD - 1, reads=[src_tb, CV])
            act(SM2.c(0, tlen), src_ap, AF.Square, reads=[src_tb], writes=[SM2])
            mm(PS3, PS3.c(0, tlen, 0, 1), CV.c(CO["ones"], 1), SM2.c(0, tlen),
               k == 0, k == KD - 1, reads=[SM2, CV])
        v_ts1(SM2.c(0, tlen, 0, 1), PS2.c(0, tlen, 0, 1), 1.0 / D, OP.mult,
              reads=[PS2], writes=[SM2])
        v_ts1(SM2.c(512, tlen, 0, 1), PS3.c(0, tlen, 0, 1), 1.0 / D, OP.mult,
              reads=[PS3], writes=[SM2])
        v_tt(SM.c(0, tlen, 0, 1), SM2.c(0, tlen, 0, 1), SM2.c(0, tlen, 0, 1),
             OP.mult, reads=[SM2], writes=[SM])
        v_tt(SM.c(0, tlen, 0, 1), SM2.c(512, tlen, 0, 1), SM.c(0, tlen, 0, 1),
             OP.subtract, reads=[SM2, SM], writes=[SM])
        v_ts1(SM.c(0, tlen, 0, 1), SM.c(0, tlen, 0, 1), EPS, OP.add,
              reads=[SM], writes=[SM])
        v_recip(SM.c(0, tlen, 0, 1), SM.c(0, tlen, 0, 1), reads=[SM], writes=[SM])
        act(SM.c(0, tlen, 0, 1), SM.c(0, tlen, 0, 1), AF.Sqrt, reads=[SM],
            writes=[SM])
        P.dma("sync", MRE.c(0, tlen), SM2.sl(0, 1, 0, [[0, 128], [1, tlen]]),
              reads=[SM2], writes=[MRE])
        P.dma("sync", RRE.c(0, tlen), SM.sl(0, 1, 0, [[0, 128], [1, tlen]]),
              reads=[SM], writes=[RRE])

    def ln_apply(src_ap, src_tb, out_ap, out_tb, tlen, gcol=None, bcol=None):
        v_tt(SM2.c(0, tlen), src_ap, MRE.c(0, tlen), OP.subtract,
             reads=[src_tb, MRE], writes=[SM2])
        if gcol is None:
            v_tt(out_ap, SM2.c(0, tlen), RRE.c(0, tlen), OP.mult,
                 reads=[SM2, RRE], writes=[out_tb])
        else:
            v_tt(SM2.c(0, tlen), SM2.c(0, tlen), RRE.c(0, tlen), OP.mult,
                 reads=[SM2, RRE], writes=[SM2])
            v_ts(out_ap, SM2.c(0, tlen), gcol, OP.mult, bcol, OP.add,
                 reads=[SM2, CV], writes=[out_tb])

    def quant_tile(dst_ap, dst_tb, src_ap, src_tb, inv_col, inv_tb, n, clip=True,
                   scratch=None):
        sc = scratch
        v_ts(sc.c(0, n), src_ap, inv_col, OP.mult, RK, OP.add,
             reads=[src_tb, inv_tb], writes=[sc])
        v_copy(ITI.c(0, n), sc.c(0, n), reads=[sc], writes=[ITI])
        v_copy(sc.c(n, n), ITI.c(0, n), reads=[ITI], writes=[sc])
        v_tt(sc.c(0, n), sc.c(0, n), sc.c(n, n), OP.subtract, reads=[sc],
             writes=[sc])
        v_stt(sc.c(0, n), sc.c(0, n), 0.5, sc.c(n, n), OP.is_ge, OP.add,
              reads=[sc], writes=[sc])
        if clip:
            v_ts(sc.c(0, n), sc.c(0, n), RK - 128.0, OP.max, RK + 127.0, OP.min,
                 reads=[sc], writes=[sc])
        v_ts1(dst_ap, sc.c(0, n), RK, OP.subtract, reads=[sc], writes=[dst_tb])

    memset(SID, 0.0, eng="gpsimd")
    def fn_id(e, a=SID.c(0, 128)):
        return e.affine_select(out=a, in_=a, compare_op=OP.not_equal, fill=1.0,
                               base=0, pattern=[[-1, 128]], channel_multiplier=1)
    P.op("gpsimd", fn_id, reads=[SID], writes=[SID])

    memset(TRI, 0.0, eng="gpsimd")
    def fn_tri(e, a=TRI.c(0, 128)):
        return e.affine_select(out=a, in_=a, compare_op=OP.is_ge, fill=1.0,
                               base=0, pattern=[[-1, 128]], channel_multiplier=1)
    P.op("gpsimd", fn_tri, reads=[TRI], writes=[TRI])
    def fn_toki(e, a=TOKI.c(0, 16)):
        return e.iota(a, pattern=[[128, 16]], base=0, channel_multiplier=1,
                      allow_small_or_imprecise_dtypes=True)
    P.op("gpsimd", fn_toki, reads=(), writes=[TOKI])
    P.dma("sync", CV.c(0, CV_W), cvec.c(0, CV_W), reads=[cvec], writes=[CV])
    P.dma("sync", X.c(0, KD * T),
          bass.AP(tensor=x_in.h, offset=0, ap=[[T, 128], [128 * T, KD], [1, T]]),
          reads=[x_in], writes=[X])
    P.dma("sync", GSC.c(0, 2, 0, 1), clips.c(0, 2, 0, 1), reads=[clips],
          writes=[GSC])
    v_copy(WBCs.c(0, 256), CV.c(CO["wbc"], 256), reads=[CV], writes=[WBCs])
    v_ts1(CV.c(CO["hb1"], KF), CV.c(CO["b1f"], KF), 0.5, OP.mult, reads=[CV],
          writes=[CV])
    v_ts1(CV.c(CO["hb2"], KD), CV.c(CO["b2f"], KD), 0.5, OP.mult, reads=[CV],
          writes=[CV])
    act(CV.c(CO["Aneg"], KD * NST), CV.c(CO["Aneg"], KD * NST), AF.Exp,
        reads=[CV], writes=[CV])
    v_ts1(CV.c(CO["Aneg"], KD * NST), CV.c(CO["Aneg"], KD * NST), -1.0, OP.mult,
          reads=[CV], writes=[CV])

    XNCb = [A2[0], XN2]
    X1C, KVB, HOB = A2[1], A2[2], A2[3]
    V3_O, V1_O, V2_O, H8_O = 0, 32, 64, 72
    RKV0 = 0
    H2U0 = 24 * TC
    AO_OFF = 0
    memset(STV, 0.0)
    PSR = [PS0, PS1, PS2, PS3]

    def lif_steps(ubuf, ubase, ngr, tc, vs_off):
        for t in range(tc):
            u_ap = ubuf.sl(0, 128, ubase + t, [[tc, ngr]])
            vs_ap = STV.c(vs_off, ngr)
            v_stt(u_ap, vs_ap, 0.5, u_ap, OP.mult, OP.add,
                  reads=[STV, ubuf], writes=[ubuf])
            v_stt(vs_ap, u_ap, 1.0, u_ap, OP.is_lt, OP.mult,
                  reads=[ubuf], writes=[STV])

    def g_tt(out_ap, a_ap, b_ap, opk, reads, writes):
        def fn(e, o=out_ap, a=a_ap, b=b_ap, k=opk):
            return e.tensor_tensor(out=o, in0=a, in1=b, op=k)
        P.op("gpsimd", fn, reads=reads, writes=writes)

    def g_ts(out_ap, in_ap, s1, o1, s2, o2, reads, writes):
        def fn(e, o=out_ap, i=in_ap, a=s1, ka=o1, b=s2, kb=o2):
            return e.tensor_scalar(out=o, in0=i, scalar1=a, scalar2=b, op0=ka, op1=kb)
        P.op("gpsimd", fn, reads=reads, writes=writes)

    def ln_apply_g(src_ap, src_tb, out_ap, out_tb, tlen, gcol=None, bcol=None):
        g_tt(GTMP.c(0, tlen), src_ap, MRE.c(0, tlen), OP.subtract,
             [src_tb, MRE], [GTMP])
        if gcol is None:
            g_tt(out_ap, GTMP.c(0, tlen), RRE.c(0, tlen), OP.mult,
                 [GTMP, RRE], [out_tb])
        else:
            g_tt(GTMP.c(0, tlen), GTMP.c(0, tlen), RRE.c(0, tlen), OP.mult,
                 [GTMP, RRE], [GTMP])
            g_ts(out_ap, GTMP.c(0, tlen), gcol, OP.mult, bcol, OP.add,
                 [GTMP, CV], [out_tb])

    def ln0_chunk(i):
        t0 = i * TC
        XNC = XNCb[i % 2]
        ln_stats(lambda k: (X.c(k * T + t0, TC), X), TC, X)
        for k in range(KD):
            ln_apply_g(X.c(k * T + t0, TC), X, XNC.c(k * TC, TC), XNC, TC)
        P.dma("sync", bass.AP(tensor=xn0_d.h, offset=t0,
                              ap=[[T, 128], [128 * T, KD], [1, TC]]),
              XNC.sl(0, 128, 0, [[TC, KD], [1, TC]]), reads=[XNC], writes=[xn0_d])
        for tt in range(TC // 128):
            for k in range(KD):
                mm(PSW0, PSW0.c(0, 8), XNC.c(k * TC + tt * 128, 128),
                   CV.c(CO["rwt"] + k * 8, 8), k == 0, k == KD - 1, reads=[XNC, CV])
            v_copy(LG.c((i * 2 + tt) * 8, 8), PSW0.c(0, 8), reads=[PSW0],
                   writes=[LG])

    def rkv_chunk(i):
        XNC = XNCb[i % 2]
        for si, wsrc in enumerate((wrt, wkt, wvt)):
            for ko in range(KD):
                g = si * KD + ko
                slot = g % 2
                load_group(WR1, slot, wsrc, 0, KD, ko * 128)
                pst = PSR[g % 4]
                for kin in range(KD):
                    mm(pst, pst.c(0, TC), rsl(WR1, slot, kin),
                       XNC.c(kin * TC, TC), kin == 0, kin == KD - 1,
                       reads=[WR1, XNC])
                act(A8a.c(RKV0 + g * TC, TC), pst.c(0, TC),
                    AF.Identity, scale=0.5, reads=[pst], writes=[A8a])

    memset(GTMP, 0.0)
    memset(SM, 0.0)
    v_ts1(SM.c(0, 1), SM.c(1, 1), 2048.0, OP.add, reads=[SM], writes=[SM])
    for e_ in range(NE):
        P.dma("gpsimd", bass.AP(tensor=list_d.h, offset=e_ * 2176 * 2,
                                ap=[[2, 128], [256, 17], [1, 2]]),
              SM.sl(0, 128, 0, [[0, 17], [1, 2]]), reads=[SM], writes=[list_d])
    P.dma("sync", bass.AP(tensor=xn_tm.h, offset=T * D, ap=[[D, 1], [1, 1023]]),
          SM.c(1, 1023, 0, 1), reads=[SM], writes=[xn_tm])
    P.dma("sync", bass.AP(tensor=xn_tm.h, offset=T * D + 1023, ap=[[D, 1], [1, 1]]),
          SM.c(1, 1, 0, 1), reads=[SM], writes=[xn_tm])

    def cprep_chunk(ci):
        for tt in (ci * 2, ci * 2 + 1):
            for k in range(KD):
                P.dma("sync", A2[2].c(k * 128, 128),
                      fm_dram(xn0_d, k, tt * 128, 128), reads=[xn0_d], writes=[A2[2]])
            for k in range(KD):
                pst = PSR[k % 4]
                tpose(pst, pst.c(0, 128), A2[2].c(k * 128, 128), reads=[A2[2]])
                v_copy(A2[3].c(k * 128, 128), pst.c(0, 128), reads=[pst],
                       writes=[A2[3]])
            P.dma("sync", bass.AP(tensor=xn_tm.h, offset=tt * 128 * D,
                                  ap=[[D, 128], [1, D]]),
                  A2[3].c(0, 1024), reads=[A2[3]], writes=[xn_tm])
        for tt in (ci * 2, ci * 2 + 1):
            v_ts1(GTMP.c(0, 8), ME.c(tt * 8, 8), 0.0, OP.is_gt, reads=[ME],
                  writes=[GTMP])
            mm(PSW1, PSW1.c(0, 8), TRI.c(0, 128), GTMP.c(0, 8), True, True,
               reads=[TRI, GTMP])
            mm(PSW0, PSW0.c(0, 8, 0, 1), CV.c(CO["ones"], 1), GTMP.c(0, 8),
               True, True, reads=[CV, GTMP])
            v_tt(GTMP.c(32, 8), PSW1.c(0, 8), STV.c(108, 8), OP.add,
                 reads=[PSW1, STV], writes=[GTMP])
            v_tt(GTMP.c(40, 8), GTMP.c(32, 8), GTMP.c(0, 8), OP.mult,
                 reads=[GTMP], writes=[GTMP])
            v_ts(GTMP.c(48, 8), GTMP.c(0, 8), -2050.0, OP.mult, 2050.0, OP.add,
                 reads=[GTMP], writes=[GTMP])
            v_tt(GTMP.c(40, 8), GTMP.c(40, 8), GTMP.c(48, 8), OP.add,
                 reads=[GTMP], writes=[GTMP])
            def fno(e, o=STV.c(116, 8, 0, 1), a=STV.c(116, 8, 0, 1),
                    b=PSW0.c(0, 8, 0, 1)):
                return e.tensor_tensor(out=o, in0=a, in1=b, op=OP.add)
            P.op("vector", fno, reads=[STV, PSW0], writes=[STV])
            P.dma("sync", STV.c(108, 8),
                  STV.sl(0, 1, 116, [[0, 128], [1, 8]]), reads=[STV], writes=[STV])
            for e_ in range(NE):
                v_copy(GTMP.c(56, 1), TOKI.c(tt, 1), reads=[TOKI], writes=[GTMP])
                v_copy(GTMP.c(57, 1), ME.c(tt * 8 + e_, 1), reads=[ME], writes=[GTMP])
                v_ts1(ITI.c(0, 1), GTMP.c(40 + e_, 1), float(e_ * 2176), OP.add,
                      reads=[GTMP], writes=[ITI])
                P.idma(bass.AP(tensor=list_d.h, offset=0, ap=[[2, 128], [1, 2]]),
                       bass.IndirectOffsetOnAxis(ITI.c(0, 1), 0),
                       GTMP.c(56, 2), None, reads=[GTMP, ITI], writes=[list_d])

    def top2_chunk(ci):
        for tt in (ci * 2, ci * 2 + 1):
            def fnm(e, o=SM.c(0, 8), i=LG.c(tt * 8, 8)):
                return e.max(out=o, in_=i)
            P.op("vector", fnm, reads=[LG], writes=[SM])
            def fni(e, o=IDXU.c(0, 8), m=SM.c(0, 8), i=LG.c(tt * 8, 8)):
                return e.max_index(out=o, in_max=m, in_values=i)
            P.op("vector", fni, reads=[SM, LG], writes=[IDXU])
            v_copy(SM.c(16, 8), IDXU.c(0, 8), reads=[IDXU], writes=[SM])
            v_tt(SM.c(8, 1), SM.c(0, 1), SM.c(1, 1), OP.subtract, reads=[SM],
                 writes=[SM])
            act(SM.c(9, 1), SM.c(8, 1), AF.Sigmoid, reads=[SM], writes=[SM])
            act(SM.c(10, 1), SM.c(8, 1), AF.Sigmoid, scale=-1.0, reads=[SM],
                writes=[SM])
            for e_ in range(NE):
                v_ts(SM.c(11, 1), SM.c(16, 1), float(e_), OP.is_equal, SM.c(9, 1),
                     OP.mult, reads=[SM], writes=[SM])
                v_ts(SM.c(12, 1), SM.c(17, 1), float(e_), OP.is_equal, SM.c(10, 1),
                     OP.mult, reads=[SM], writes=[SM])
                v_tt(ME.c(tt * 8 + e_, 1), SM.c(11, 1), SM.c(12, 1), OP.add,
                     reads=[SM], writes=[ME])

    ln0_chunk(0)
    rkv_chunk(0)
    for i in range(NCH):
        t0 = i * TC
        XNC = XNCb[i % 2]
        if i + 1 < NCH:
            ln0_chunk(i + 1)
        top2_chunk(i)
        cprep_chunk(i)
        lif_steps(A8a, RKV0, 24, TC, V3_O)
        v_ts1(A8a.c(RKV0, 24 * TC), A8a.c(RKV0, 24 * TC), 1.0, OP.is_ge,
              reads=[A8a], writes=[A8a])
        v_tt(KVB.c(0, KD * TC), A8a.c(RKV0 + 8 * TC, KD * TC),
             A8a.c(RKV0 + 16 * TC, KD * TC), OP.mult, reads=[A8a], writes=[KVB])
        for j in range(KD):
            def fnsc(e, o=HOB.c(j * TC, TC), d0=CV.c(CO["dec"], TC),
                     d1=KVB.c(j * TC, TC), ini=STV.c(H8_O + j, 1)):
                return e.tensor_tensor_scan(out=o, data0=d0, data1=d1,
                                            initial=ini, op0=OP.mult, op1=OP.add)
            P.op("vector", fnsc, reads=[CV, KVB, STV], writes=[HOB])
            v_copy(STV.c(H8_O + j, 1), HOB.c(j * TC + TC - 1, 1), reads=[HOB],
                   writes=[STV])
        v_tt(KVB.c(0, KD * TC), A8a.c(RKV0, KD * TC), HOB.c(0, KD * TC), OP.mult,
             reads=[A8a, HOB], writes=[KVB])
        if i + 1 < NCH:
            rkv_chunk(i + 1)
        for ko in range(KD):
            slot = ko % 2
            load_group(WR1, slot, wot, 0, KD, ko * 128)
            pst = PS0 if slot == 0 else PS1
            for kin in range(KD):
                mm(pst, pst.c(0, TC), rsl(WR1, slot, kin), KVB.c(kin * TC, TC),
                   kin == 0, kin == KD - 1, reads=[WR1, KVB])
            act(A8b.c(AO_OFF + ko * TC, TC), pst.c(0, TC), AF.Identity,
                reads=[pst], writes=[A8b])
        g_tt(A8b.c(AO_OFF, KD * TC), XNC.c(0, KD * TC), A8b.c(AO_OFF, KD * TC),
             OP.add, [XNC, A8b], [A8b])
        ln_stats(lambda k: (A8b.c(AO_OFF + k * TC, TC), A8b), TC, A8b)
        for k in range(KD):
            ln_apply_g(A8b.c(AO_OFF + k * TC, TC), A8b, X1C.c(k * TC, TC), X1C,
                       TC, gcol=CV.c(CO["ln1g"] + k, 1),
                       bcol=CV.c(CO["ln1b"] + k, 1))
        for fo in range(KF):
            slot = fo % 2
            load_group(WR1, slot, w1t, 0, KD, fo * 128)
            pst = PS0 if slot == 0 else PS1
            for kin in range(KD):
                mm(pst, pst.c(0, TC), rsl(WR1, slot, kin), X1C.c(kin * TC, TC),
                   kin == 0, kin == KD - 1, reads=[WR1, X1C])
            act(A8b.c(fo * TC, TC), pst.c(0, TC), AF.Identity, scale=0.5,
                bias=CV.c(CO["hb1"] + fo, 1), reads=[pst, CV], writes=[A8b])
        lif_steps(A8b, 0, KF, TC, V1_O)
        v_ts1(A8b.c(0, KF * TC), A8b.c(0, KF * TC), 1.0, OP.is_ge, reads=[A8b],
              writes=[A8b])
        for ko in range(KD):
            pst = PS2 if ko % 2 == 0 else PS3
            for gi in range(4):
                slot = gi % 2
                load_group(WR2, slot, w2t, gi * 8 * 128, 8, ko * 128)
                for kin in range(8):
                    kk = gi * 8 + kin
                    mm(pst, pst.c(0, TC), rsl(WR2, slot, kin),
                       A8b.c(kk * TC, TC), kk == 0, kk == KF - 1,
                       reads=[WR2, A8b])
            act(A8a.c(H2U0 + ko * TC, TC), pst.c(0, TC), AF.Identity, scale=0.5,
                bias=CV.c(CO["hb2"] + ko, 1), reads=[pst, CV], writes=[A8a])
        lif_steps(A8a, H2U0, KD, TC, V2_O)
        v_ts1(A8a.c(H2U0, KD * TC), A8a.c(H2U0, KD * TC), 1.0, OP.is_ge,
              reads=[A8a], writes=[A8a])
        g_tt(A8a.c(H2U0, KD * TC), X1C.c(0, KD * TC), A8a.c(H2U0, KD * TC),
             OP.add, [X1C, A8a], [A8a])
        ln_stats(lambda k: (A8a.c(H2U0 + k * TC, TC), A8a), TC, A8a)
        for k in range(KD):
            ln_apply_g(A8a.c(H2U0 + k * TC, TC), A8a, KVB.c(k * TC, TC), KVB, TC,
                       gcol=CV.c(CO["ln2g"] + k, 1), bcol=CV.c(CO["ln2b"] + k, 1))
            g_tt(X.c(k * T + t0, TC), X.c(k * T + t0, TC), KVB.c(k * TC, TC),
                 OP.add, [X, KVB], [X])

    if DEBUG:
        dbg_xB = TB(nc.dram_tensor("dbg_xB", [D, T], F32, kind="ExternalOutput"),
                    "dbg_xB", T)
        P.dma("sync", bass.AP(tensor=dbg_xB.h, offset=0,
                              ap=[[T, 128], [128 * T, KD], [1, T]]),
              X.c(0, KD * T), reads=[X], writes=[dbg_xB])

    if STOP_AFTER == "B0":
        P.dma("sync", bass.AP(tensor=y_out.h, offset=0,
                              ap=[[T, 128], [128 * T, KD], [1, T]]),
              X.c(0, KD * T), reads=[X], writes=[y_out])
        P.barrier_wait("sync", [y_out])
        P.finalize()
        return P.nc

    B2R = SM2
    P.dma("sync", B2R.c(0, D, 0, NE), b2moe.c(0, D, 0, NE), reads=[b2moe],
          writes=[B2R])
    for tt in range(T // 128):
        tpose(PS2, PS2.sl(0, 8, 0, [[1, 128]]), ME.sl(0, 128, tt * 8, [[1, 8]]),
              reads=[ME])
        v_copy(GTMP.sl(0, 8, 0, [[1, 128]]), PS2.sl(0, 8, 0, [[1, 128]]),
               reads=[PS2], writes=[GTMP])
        for hf in range(2):
            mm(PSW0, PSW0.c(hf * 512, 512), GTMP.sl(0, 8, 0, [[1, 128]]),
               B2R.c(hf * 512, 512, 0, NE), True, True, reads=[GTMP, B2R])
        v_copy(SM.c(0, 1024), PSW0.c(0, 1024), reads=[PSW0], writes=[SM])
        P.dma("sync", bass.AP(tensor=moe_d.h, offset=tt * 128 * D,
                              ap=[[D, 128], [1, D]]),
              SM.c(0, 1024), reads=[SM], writes=[moe_d])
    CAPB = (512, 256)
    CAP0 = (0, 512)
    for e_ in range(NE):
        for blk in range(2):
            nslot = CAPB[blk]
            s0 = CAP0[blk]
            ntt = nslot // 128
            for q in range(ntt):
                P.dma("sync", A2[3].c(q * 2, 2),
                      bass.AP(tensor=list_d.h,
                              offset=(e_ * 2176 + s0 + q * 128) * 2,
                              ap=[[2, 128], [1, 2]]),
                      reads=[list_d], writes=[A2[3]])
            v_copy(ITI.c(0, ntt), A2[3].sl(0, 128, 0, [[2, ntt]]),
                   reads=[A2[3]], writes=[ITI])
            for q in range(ntt):
                P.idma(A2[2].c(0, 1024),
                       None,
                       bass.AP(tensor=xn_tm.h, offset=0, ap=[[D, 128], [1, D]]),
                       bass.IndirectOffsetOnAxis(ITI.c(q, 1), 0),
                       reads=[xn_tm, ITI], writes=[A2[2]])
                for k in range(KD):
                    pst = PSR[k % 4]
                    tpose(pst, pst.c(0, 128), A2[2].c(k * 128, 128),
                          reads=[A2[2]])
                    dst = A2[0] if k < 4 else A2[1]
                    v_copy(dst.c((k % 4) * nslot + q * 128, 128), pst.c(0, 128),
                           reads=[pst], writes=[dst])
            for fo in range(KF):
                slot = fo % 2
                load_group(WR1, slot, mw1t, e_ * D, KD, fo * 128)
                pst = PS0 if slot == 0 else PS1
                for kin in range(KD):
                    srcb = A2[0] if kin < 4 else A2[1]
                    mm(pst, pst.c(0, nslot), rsl(WR1, slot, kin),
                       srcb.c((kin % 4) * nslot, nslot), kin == 0, kin == KD - 1,
                       reads=[WR1, srcb])
                gdst = A8a if fo < 16 else A8b
                act(gdst.c((fo % 16) * nslot, nslot), pst.c(0, nslot), AF.Gelu,
                    bias=CV.c(CO["moeb1"] + e_ * KF + fo, 1), reads=[pst, CV],
                    writes=[gdst])
            accs = [[(PSW0, PSW0.c(0, 512)), (PSW0, PSW0.c(512, 512))],
                    [(PSW1, PSW1.c(0, 512)), (PSW1, PSW1.c(512, 512))],
                    [(PS0, PS0.c(0, 512)), (PS1, PS1.c(0, 512))],
                    [(PS2, PS2.c(0, 512)), (PS3, PS3.c(0, 512))]]
            for kf in range(KF):
                slot = kf % 2
                P.dma("sync", WR2.c(slot * 1024, 1024),
                      bass.AP(tensor=mw2t.h, offset=(e_ * FF + kf * 128) * D,
                              ap=[[D, 128], [1, D]]),
                      reads=[mw2t], writes=[WR2])
                gsrc = A8a if kf < 16 else A8b
                for q in range(ntt):
                    for hf in range(2):
                        pt, oap = accs[q][hf]
                        mm(pt, oap, gsrc.c((kf % 16) * nslot + q * 128, 128),
                           WR2.sl(0, 128, slot * 1024 + hf * 512, [[1, 512]]),
                           kf == 0, kf == KF - 1, reads=[WR2, gsrc])
            for q in range(ntt):
                for hf in range(2):
                    pt, oap = accs[q][hf]
                    v_ts1(SM.c(hf * 512, 512), oap, A2[3].c(q * 2 + 1, 1),
                          OP.mult, reads=[pt, A2[3]], writes=[SM])
                P.idma(bass.AP(tensor=moe_d.h, offset=0, ap=[[D, 128], [1, D]]),
                       bass.IndirectOffsetOnAxis(ITI.c(q, 1), 0),
                       SM.c(0, 1024), None, accum=True,
                       reads=[SM, ITI], writes=[moe_d])

    for tt in range(T // 128):
        P.dma("sync", A2[2].c(0, 1024),
              bass.AP(tensor=moe_d.h, offset=tt * 128 * D, ap=[[D, 128], [1, D]]),
              reads=[moe_d], writes=[A2[2]])
        for k in range(KD):
            tpose(PS2, PS2.c(0, 128), A2[2].c(k * 128, 128), reads=[A2[2]])
            v_tt(X.c(k * T + tt * 128, 128), X.c(k * T + tt * 128, 128),
                 PS2.c(0, 128), OP.add, reads=[X, PS2], writes=[X])

    if DEBUG:
        dbg_xC = TB(nc.dram_tensor("dbg_xC", [D, T], F32, kind="ExternalOutput"),
                    "dbg_xC", T)
        P.dma("sync", bass.AP(tensor=dbg_xC.h, offset=0,
                              ap=[[T, 128], [128 * T, KD], [1, T]]),
              X.c(0, KD * T), reads=[X], writes=[dbg_xC])

    for c in range(NCH):
        t0 = c * TC
        ln_stats(lambda k: (X.c(k * T + t0, TC), X), TC, X)
        for k in range(KD):
            ln_apply(X.c(k * T + t0, TC), X, XNC.c(k * TC, TC), XNC, TC)
        P.dma("sync", bass.AP(tensor=xn1_d.h, offset=t0,
                              ap=[[T, 128], [128 * T, KD], [1, TC]]),
              XNC.sl(0, 128, 0, [[TC, KD], [1, TC]]), reads=[XNC], writes=[xn1_d])

    def xc_tb(k):
        return (A8a, (k % 4) * 2048) if k < 4 else (A8b, (k % 4) * 2048)

    for ts_ in range(4):
        t0 = ts_ * 512
        for kin in range(KD):
            dst = A2[kin // 4]
            P.dma("sync", dst.c((kin % 4) * 512, 512), fm_dram(xn1_d, kin, t0, 512),
                  reads=[xn1_d], writes=[dst])
        for jo in range(16):
            slot = jo % 2
            load_group(WR1, slot, w_in, 0, KD, jo * 128)
            pst = PS0 if jo % 2 == 0 else PS1
            for kin in range(KD):
                src = A2[kin // 4]
                mm(pst, pst.c(0, 512), rsl(WR1, slot, kin),
                   src.c((kin % 4) * 512, 512), kin == 0, kin == KD - 1,
                   reads=[WR1, src])
            if jo < 8:
                tbx, xoff = xc_tb(jo)
                act(tbx.c(xoff + t0, 512), pst.c(0, 512), AF.Identity,
                    bias=CV.c(CO["bin"] + jo, 1), reads=[pst, CV], writes=[tbx])
            else:
                act(SM.c(0, 512), pst.c(0, 512), AF.Silu,
                    bias=CV.c(CO["bin"] + jo, 1), reads=[pst, CV], writes=[SM])
                P.dma("sync", fm_dram(xg_d, jo - 8, t0, 512), SM.c(0, 512),
                      reads=[SM], writes=[xg_d])
    for k in range(KD):
        tbx, xoff = xc_tb(k)
        v_ts1(A2[0].c(0, T), tbx.c(xoff, T), CV.c(CO["convw"] + k * 4 + 3, 1),
              OP.mult, reads=[tbx, CV], writes=[A2[0]])
        for j in (2, 1, 0):
            sh = 3 - j
            v_stt(A2[0].c(sh, T - sh), tbx.c(xoff, T - sh),
                  CV.c(CO["convw"] + k * 4 + j, 1), A2[0].c(sh, T - sh),
                  OP.mult, OP.add, reads=[tbx, CV, A2[0]], writes=[A2[0]])
        act(tbx.c(xoff, T), A2[0].c(0, T), AF.Silu, bias=CV.c(CO["convb"] + k, 1),
            reads=[A2[0], CV], writes=[tbx])
    for ts_ in range(4):
        t0 = ts_ * 512
        for kin in range(KD):
            tbx, xoff = xc_tb(kin)
            mm(PS2, PS2.c(0, 512, 0, 32), WBCs.c(kin * 32, 32),
               tbx.c(xoff + t0, 512), kin == 0, kin == KD - 1, reads=[WBCs, tbx])
        def fnb(e, o=SM.sl(0, 32, 0, [[1, 512]]), i=PS2.sl(0, 32, 0, [[1, 512]]),
                b=CV.sl(0, 32, CO["bbc"], [[1, 1]])):
            return e.activation(out=o, in_=i, func=AF.Identity, bias=b, scale=1.0)
        P.op("scalar", fnb, reads=[PS2, CV], writes=[SM])
        P.dma("sync", bass.AP(tensor=bc_d.h, offset=t0, ap=[[T, 32], [1, 512]]),
              SM.sl(0, 32, 0, [[1, 512]]), reads=[SM], writes=[bc_d])
    for ts_ in range(4):
        t0 = ts_ * 512
        for jo in range(KD):
            slot = jo % 2
            load_group(WR1, slot, w_dt, 0, KD, jo * 128)
            pst = PS0 if jo % 2 == 0 else PS1
            for kin in range(KD):
                tbx, xoff = xc_tb(kin)
                mm(pst, pst.c(0, 512), rsl(WR1, slot, kin),
                   tbx.c(xoff + t0, 512), kin == 0, kin == KD - 1,
                   reads=[WR1, tbx])
            act(SM.c(0, 512), pst.c(0, 512), AF.Identity,
                bias=CV.c(CO["bdt"] + jo, 1), reads=[pst, CV], writes=[SM])
            act(SM.c(512, 512), SM.c(0, 512), AF.Abs, reads=[SM], writes=[SM])
            act(SM.c(512, 512), SM.c(512, 512), AF.Exp, scale=-1.0, reads=[SM],
                writes=[SM])
            act(SM.c(512, 512), SM.c(512, 512), AF.Ln, bias=1.0, reads=[SM],
                writes=[SM])
            def fnre(e, o=SM2.c(0, 512), i=SM.c(0, 512)):
                return e.tensor_relu(out=o, in_=i)
            P.op("vector", fnre, reads=[SM], writes=[SM2])
            v_tt(SM.c(0, 512), SM2.c(0, 512), SM.c(512, 512), OP.add,
                 reads=[SM, SM2], writes=[SM])
            P.dma("sync", fm_dram(dt_d, jo, t0, 512), SM.c(0, 512), reads=[SM],
                  writes=[dt_d])
    memset(MST, 0.0)

    def e5_chunk(cs_):
        t0 = cs_ * LM
        for n in range(NST):
            P.dma("sync", A2[1].c(n * LM, LM),
                  bass.AP(tensor=bc_d.h, offset=n * T + t0, ap=[[0, 128], [1, LM]]),
                  reads=[bc_d], writes=[A2[1]])
            P.dma("sync", A2[2].c(n * LM, LM),
                  bass.AP(tensor=bc_d.h, offset=(NST + n) * T + t0,
                          ap=[[0, 128], [1, LM]]),
                  reads=[bc_d], writes=[A2[2]])
        P.dma("sync", SM.c(0, KD * LM),
              bass.AP(tensor=dt_d.h, offset=t0,
                      ap=[[T, 128], [128 * T, KD], [1, LM]]),
              reads=[dt_d], writes=[SM])
        for k in range(KD):
            tbx, xoff = xc_tb(k)
            v_tt(SM2.c(128, LM), SM.c(k * LM, LM), tbx.c(xoff + t0, LM), OP.mult,
                 reads=[SM, tbx], writes=[SM2])
            def fnp(e, o=A2[0].sl(0, 128, 0, [[LM, NST], [1, LM]]),
                    i0=SM.sl(0, 128, k * LM, [[0, NST], [1, LM]]),
                    i1=CV.sl(0, 128, CO["Aneg"] + k * NST, [[1, NST], [0, LM]])):
                return e.tensor_tensor(out=o, in0=i0, in1=i1, op=OP.mult)
            P.op("vector", fnp, reads=[SM, CV], writes=[A2[0]])
            act(A2[0].c(0, NST * LM), A2[0].c(0, NST * LM), AF.Exp,
                reads=[A2[0]], writes=[A2[0]])
            def fnu(e, o=A2[3].sl(0, 128, 0, [[LM, NST], [1, LM]]),
                    i0=SM2.sl(0, 128, 128, [[0, NST], [1, LM]]),
                    i1=A2[1].sl(0, 128, 0, [[LM, NST], [1, LM]])):
                return e.tensor_tensor(out=o, in0=i0, in1=i1, op=OP.mult)
            P.op("vector", fnu, reads=[SM2, A2[1]], writes=[A2[3]])
            for n in range(NST):
                def fns(e, o=SM2.c(256, LM), d0=A2[0].c(n * LM, LM),
                        d1=A2[3].c(n * LM, LM), ini=MST.c(k * NST + n, 1)):
                    return e.tensor_tensor_scan(out=o, data0=d0, data1=d1,
                                                initial=ini, op0=OP.mult,
                                                op1=OP.add)
                P.op("vector", fns, reads=[A2[0], A2[3], MST], writes=[SM2])
                v_copy(MST.c(k * NST + n, 1), SM2.c(256 + LM - 1, 1),
                       reads=[SM2], writes=[MST])
                v_tt(A2[3].c(n * LM, LM), SM2.c(256, LM), A2[2].c(n * LM, LM),
                     OP.mult, reads=[SM2, A2[2]], writes=[A2[3]])
            def fnr2(e, o=SM2.c(384, LM),
                     i=A2[3].sl(0, 128, 0, [[1, LM], [LM, NST]])):
                return e.tensor_reduce(out=o, in_=i, axis=AXX, op=OP.add)
            P.op("vector", fnr2, reads=[A2[3]], writes=[SM2])
            v_stt(SM2.c(384, LM), tbx.c(xoff + t0, LM), CV.c(CO["Dv"] + k, 1),
                  SM2.c(384, LM), OP.mult, OP.add, reads=[tbx, CV, SM2],
                  writes=[SM2])
            P.dma("sync", SM2.c(512, LM), fm_dram(xg_d, k, t0, LM),
                  reads=[xg_d], writes=[SM2])
            v_tt(SM2.c(640, LM), SM2.c(384, LM), SM2.c(512, LM), OP.mult,
                 reads=[SM2], writes=[SM2])
            P.dma("sync", fm_dram(ys_d, k, t0, LM), SM2.c(640, LM), reads=[SM2],
                  writes=[ys_d])
    def e6_ts(ts_):
        t0 = ts_ * 512
        for jo in range(KD):
            slot = jo % 2
            load_group(WR1, slot, w_out, 0, KD, jo * 128)
            pst = PS0 if jo % 2 == 0 else PS1
            for kin in range(KD):
                P.dma("sync", WR2.c((kin % 4) * 512, 512),
                      fm_dram(ys_d, kin, t0, 512), reads=[ys_d], writes=[WR2])
                mm(pst, pst.c(0, 512), rsl(WR1, slot, kin),
                   WR2.c((kin % 4) * 512, 512), kin == 0, kin == KD - 1,
                   reads=[WR1, WR2])
            tbr, roff = xc_tb(jo)
            act(SM.c(0, 512), pst.c(0, 512), AF.Identity,
                bias=CV.c(CO["bout"] + jo, 1), reads=[pst, CV], writes=[SM])
            P.dma("sync", SM.c(512, 512), fm_dram(xn1_d, jo, t0, 512),
                  reads=[xn1_d], writes=[SM])
            v_tt(tbr.c(roff + t0, 512), SM.c(0, 512), SM.c(512, 512),
                 OP.add, reads=[SM], writes=[tbr])
        ln_stats(lambda k: (xc_tb(k)[0].c(xc_tb(k)[1] + t0, 512), xc_tb(k)[0]),
                 512, None)
        for k in range(KD):
            tbr, roff = xc_tb(k)
            ln_apply(tbr.c(roff + t0, 512), tbr, SM.c(0, 512), SM, 512,
                     gcol=CV.c(CO["lngm"] + k, 1), bcol=CV.c(CO["lnbm"] + k, 1))
            v_tt(X.c(k * T + t0, 512), X.c(k * T + t0, 512), SM.c(0, 512),
                 OP.add, reads=[X, SM], writes=[X])

    for ts_ in range(4):
        for cs_ in range(4 * ts_, 4 * ts_ + 4):
            e5_chunk(cs_)
        e6_ts(ts_)

    if DEBUG:
        dbg_xE = TB(nc.dram_tensor("dbg_xE", [D, T], F32, kind="ExternalOutput"),
                    "dbg_xE", T)
        P.dma("sync", bass.AP(tensor=dbg_xE.h, offset=0,
                              ap=[[T, 128], [128 * T, KD], [1, T]]),
              X.c(0, KD * T), reads=[X], writes=[dbg_xE])

    for k in range(KD):
        P.dma("sync", A2[0].c(0, T), fm_dram(xn1_d, k, 0, T), reads=[xn1_d],
              writes=[A2[0]])
        v_red(STV.c(96 + k, 1), A2[0].c(0, T), OP.max, reads=[A2[0]],
              writes=[STV], absv=True)
    v_red(STV.c(104, 1), STV.c(96, 8), OP.max, reads=[STV], writes=[STV])
    tpose(PS2, PS2.sl(0, 1, 0, [[1, 128]]), STV.sl(0, 128, 104, [[1, 1]]),
          reads=[STV])
    v_copy(SM.c(0, 128, 0, 1), PS2.c(0, 128, 0, 1), reads=[PS2], writes=[SM])
    v_red(GSC.c(2, 1, 0, 1), SM.c(0, 128, 0, 1), OP.max, reads=[SM], writes=[GSC])
    v_copy(GSC.c(10, 4, 0, 1), GSC.sl(0, 1, 2, [[0, 4]]), reads=[GSC],
           writes=[GSC])
    P.dma("gpsimd", cc_in1.c(0, 4, 0, 1), GSC.c(10, 4, 0, 1),
          reads=[GSC], writes=[cc_in1])
    def fncc1(e, i=cc_in1.ap().opt(), o=cc_out1.ap().opt()):
        return e.collective_compute("AllReduce", OP.max,
                                    replica_groups=[list(range(NCORE))],
                                    ins=[i], outs=[o])
    P.op("gpsimd", fncc1, reads=[cc_in1], writes=[cc_out1])
    P.dma("gpsimd", GSC.c(3, 1, 0, 1), cc_out1.c(0, 1, 0, 1), reads=[cc_out1],
          writes=[GSC])
    act(GSC.c(4, 1, 0, 1), GSC.c(0, 1, 0, 1), AF.Sigmoid, reads=[GSC],
        writes=[GSC])
    v_tt(GSC.c(4, 1, 0, 1), GSC.c(4, 1, 0, 1), GSC.c(3, 1, 0, 1), OP.mult,
         reads=[GSC], writes=[GSC])
    v_ts1(GSC.c(4, 1, 0, 1), GSC.c(4, 1, 0, 1), R127, OP.mult, reads=[GSC],
          writes=[GSC])
    v_recip(GSC.c(5, 1, 0, 1), GSC.c(4, 1, 0, 1), reads=[GSC], writes=[GSC])
    P.dma("sync", SXR.c(0, 1), GSC.sl(0, 1, 5, [[0, 128], [1, 1]]), reads=[GSC],
          writes=[SXR])
    P.dma("sync", SXR.c(1, 1), GSC.sl(0, 1, 4, [[0, 128], [1, 1]]), reads=[GSC],
          writes=[SXR])
    for k in range(KD):
        tbx, xoff = xc_tb(k)
        for hh in range(2):
            P.dma("sync", A2[1].c(0, 1024), fm_dram(xn1_d, k, hh * 1024, 1024),
                  reads=[xn1_d], writes=[A2[1]])
            for qq in range(2):
                quant_tile(tbx.c(xoff + hh * 1024 + qq * 512, 512), tbx,
                           A2[1].c(qq * 512, 512), A2[1], SXR.c(0, 1), SXR, 512,
                           clip=True, scratch=A2[0])
    for ic in range(KI):
        P.dma("sync", SM.c(0, 1024),
              bass.AP(tensor=wg_raw.h, offset=ic * 128 * D, ap=[[D, 128], [1, D]]),
              reads=[wg_raw], writes=[SM])
        v_red(SM2.c(0, 1), SM.c(0, 1024), OP.max, reads=[SM], writes=[SM2],
              absv=True)
        v_ts1(SM2.c(1, 1), SM2.c(0, 1), R127, OP.mult, reads=[SM2], writes=[SM2])
        v_recip(SM2.c(2, 1), SM2.c(1, 1), reads=[SM2], writes=[SM2])
        v_tt(SM2.c(3, 1), SM2.c(1, 1), SXR.c(1, 1), OP.mult, reads=[SM2, SXR],
             writes=[SM2])
        for qq in range(2):
            quant_tile(SM.c(qq * 512, 512), SM, SM.c(qq * 512, 512), SM,
                       SM2.c(2, 1), SM2, 512, clip=True, scratch=A2[0])
        for hk in range(KD):
            tpose(PS2, PS2.c(0, 128), SM.c(hk * 128, 128), reads=[SM])
            v_copy(A2[1].c(hk * 128, 128), PS2.c(0, 128), reads=[PS2],
                   writes=[A2[1]])
        for ts_ in range(4):
            for hk in range(KD):
                tbxq, xqoff = xc_tb(hk)
                mm(PS0, PS0.c(0, 512), A2[1].c(hk * 128, 128),
                   tbxq.c(xqoff + ts_ * 512, 512), hk == 0, hk == KD - 1,
                   reads=[A2[1], tbxq])
            act(A2[2].c(ts_ * 512, 512), PS0.c(0, 512), AF.Silu,
                scale=SM2.c(3, 1), reads=[PS0, SM2], writes=[A2[2]])
        P.dma("sync", SM.c(0, 1024),
              bass.AP(tensor=wu_raw.h, offset=ic * 128 * D, ap=[[D, 128], [1, D]]),
              reads=[wu_raw], writes=[SM])
        v_red(SM2.c(4, 1), SM.c(0, 1024), OP.max, reads=[SM], writes=[SM2],
              absv=True)
        v_ts1(SM2.c(5, 1), SM2.c(4, 1), R127, OP.mult, reads=[SM2], writes=[SM2])
        v_recip(SM2.c(6, 1), SM2.c(5, 1), reads=[SM2], writes=[SM2])
        v_tt(SM2.c(7, 1), SM2.c(5, 1), SXR.c(1, 1), OP.mult, reads=[SM2, SXR],
             writes=[SM2])
        for qq in range(2):
            quant_tile(SM.c(qq * 512, 512), SM, SM.c(qq * 512, 512), SM,
                       SM2.c(6, 1), SM2, 512, clip=True, scratch=A2[0])
        for hk in range(KD):
            tpose(PS2, PS2.c(0, 128), SM.c(hk * 128, 128), reads=[SM])
            v_copy(A2[1].c(hk * 128, 128), PS2.c(0, 128), reads=[PS2],
                   writes=[A2[1]])
        for ts_ in range(4):
            for hk in range(KD):
                tbxq, xqoff = xc_tb(hk)
                mm(PS1, PS1.c(0, 512), A2[1].c(hk * 128, 128),
                   tbxq.c(xqoff + ts_ * 512, 512), hk == 0, hk == KD - 1,
                   reads=[A2[1], tbxq])
            act(A2[3].c(ts_ * 512, 512), PS1.c(0, 512), AF.Identity,
                scale=SM2.c(7, 1), reads=[PS1, SM2], writes=[A2[3]])
        v_tt(A2[2].c(0, T), A2[2].c(0, T), A2[3].c(0, T), OP.mult,
             reads=[A2[2], A2[3]], writes=[A2[2]])
        v_red(STV.c(ic, 1), A2[2].c(0, T), OP.max, reads=[A2[2]], writes=[STV],
              absv=True)
        P.dma("sync", bass.AP(tensor=h_d.h, offset=ic * 128 * T,
                              ap=[[T, 128], [1, T]]),
              A2[2].c(0, T), reads=[A2[2]], writes=[h_d])
    v_red(STV.c(105, 1), STV.c(0, KI), OP.max, reads=[STV], writes=[STV])
    tpose(PS2, PS2.sl(0, 1, 0, [[1, 128]]), STV.sl(0, 128, 105, [[1, 1]]),
          reads=[STV])
    v_copy(SM.c(0, 128, 0, 1), PS2.c(0, 128, 0, 1), reads=[PS2], writes=[SM])
    v_red(GSC.c(6, 1, 0, 1), SM.c(0, 128, 0, 1), OP.max, reads=[SM], writes=[GSC])
    v_copy(GSC.c(10, 4, 0, 1), GSC.sl(0, 1, 6, [[0, 4]]), reads=[GSC],
           writes=[GSC])
    P.dma("gpsimd", cc_in2.c(0, 4, 0, 1), GSC.c(10, 4, 0, 1),
          reads=[GSC], writes=[cc_in2])
    def fncc2(e, i=cc_in2.ap().opt(), o=cc_out2.ap().opt()):
        return e.collective_compute("AllReduce", OP.max,
                                    replica_groups=[list(range(NCORE))],
                                    ins=[i], outs=[o])
    P.op("gpsimd", fncc2, reads=[cc_in2], writes=[cc_out2])
    P.dma("gpsimd", GSC.c(7, 1, 0, 1), cc_out2.c(0, 1, 0, 1), reads=[cc_out2],
          writes=[GSC])
    act(GSC.c(8, 1, 0, 1), GSC.c(1, 1, 0, 1), AF.Sigmoid, reads=[GSC],
        writes=[GSC])
    v_tt(GSC.c(8, 1, 0, 1), GSC.c(8, 1, 0, 1), GSC.c(7, 1, 0, 1), OP.mult,
         reads=[GSC], writes=[GSC])
    v_ts1(GSC.c(8, 1, 0, 1), GSC.c(8, 1, 0, 1), R127, OP.mult, reads=[GSC],
          writes=[GSC])
    v_recip(GSC.c(9, 1, 0, 1), GSC.c(8, 1, 0, 1), reads=[GSC], writes=[GSC])
    P.dma("sync", SXR.c(2, 1), GSC.sl(0, 1, 9, [[0, 128], [1, 1]]), reads=[GSC],
          writes=[SXR])
    P.dma("sync", SXR.c(3, 1), GSC.sl(0, 1, 8, [[0, 128], [1, 1]]), reads=[GSC],
          writes=[SXR])
    for ic in range(KI):
        for hh in range(2):
            P.dma("sync", A2[1].c(0, 1024),
                  bass.AP(tensor=h_d.h, offset=ic * 128 * T + hh * 1024,
                          ap=[[T, 128], [1, 1024]]),
                  reads=[h_d], writes=[A2[1]])
            for qq in range(2):
                quant_tile(A2[1].c(qq * 512, 512), A2[1], A2[1].c(qq * 512, 512),
                           A2[1], SXR.c(2, 1), SXR, 512, clip=True,
                           scratch=A2[0])
            P.dma("sync", bass.AP(tensor=hq_d.h, offset=ic * 128 * T + hh * 1024,
                                  ap=[[T, 128], [1, 1024]]),
                  A2[1].c(0, 1024), reads=[A2[1]], writes=[hq_d])
    for dch in range(KD):
        P.dma("sync", A8a.c(0, FI),
              bass.AP(tensor=wd_raw.h, offset=dch * 128 * FI,
                      ap=[[FI, 128], [1, FI]]),
              reads=[wd_raw], writes=[A8a])
        v_red(SM2.c(0, 1), A8a.c(0, FI), OP.max, reads=[A8a], writes=[SM2],
              absv=True)
        v_ts1(SM2.c(1, 1), SM2.c(0, 1), R127, OP.mult, reads=[SM2], writes=[SM2])
        v_recip(SM2.c(2, 1), SM2.c(1, 1), reads=[SM2], writes=[SM2])
        v_tt(SM2.c(3, 1), SM2.c(1, 1), SXR.c(3, 1), OP.mult, reads=[SM2, SXR],
             writes=[SM2])
        for qq in range((FI + 511) // 512):
            w = min(512, FI - qq * 512)
            quant_tile(A8a.c(qq * 512, w), A8a, A8a.c(qq * 512, w), A8a,
                       SM2.c(2, 1), SM2, w, clip=True, scratch=A2[0])
        for ik in range(KI):
            tpose(PS2, PS2.c(0, 128), A8a.c(ik * 128, 128), reads=[A8a])
            v_copy(A8a.c(4096 + ik * 128, 128), PS2.c(0, 128), reads=[PS2],
                   writes=[A8a])
        for ts_ in range(4):
            t0 = ts_ * 512
            for ik in range(KI):
                P.dma("sync", A2[1].c(0, 512),
                      bass.AP(tensor=hq_d.h, offset=ik * 128 * T + t0,
                              ap=[[T, 128], [1, 512]]),
                      reads=[hq_d], writes=[A2[1]])
                mm(PS0, PS0.c(0, 512), A8a.c(4096 + ik * 128, 128),
                   A2[1].c(0, 512), ik == 0, ik == KI - 1, reads=[A8a, A2[1]])
            act(SM.c(0, 512), PS0.c(0, 512), AF.Identity, scale=SM2.c(3, 1),
                reads=[PS0, SM2], writes=[SM])
            v_tt(X.c(dch * T + t0, 512), X.c(dch * T + t0, 512), SM.c(0, 512),
                 OP.add, reads=[X, SM], writes=[X])

    if DEBUG:
        for nm, src, shp in [("dbg_xn0", xn0_d, (D, T)), ("dbg_moe", moe_d, (T, D)),
                             ("dbg_xn1", xn1_d, (D, T)), ("dbg_xg", xg_d, (D, T)),
                             ("dbg_dt", dt_d, (D, T)), ("dbg_bc", bc_d, (32, T)),
                             ("dbg_ys", ys_d, (D, T)), ("dbg_h", h_d, (FI, T)),
                             ("dbg_hq", hq_d, (FI, T))]:
            dtb = TB(nc.dram_tensor(nm, list(shp), F32, kind="ExternalOutput"),
                     nm, shp[1])
            P.dma("sync", dtb.ap(), src.ap(), reads=[src], writes=[dtb])
        dgs = TB(nc.dram_tensor("dbg_gsc", [1, 16], F32, kind="ExternalOutput"),
                 "dbg_gsc", 16)
        P.dma("sync", dgs.c(0, 16, 0, 1), GSC.c(0, 16, 0, 1), reads=[GSC],
              writes=[dgs])
        dlg = TB(nc.dram_tensor("dbg_lg", [128, 128], F32, kind="ExternalOutput"),
                 "dbg_lg", 128)
        P.dma("sync", dlg.ap(), LG.c(0, 128), reads=[LG], writes=[dlg])
        dme = TB(nc.dram_tensor("dbg_me", [128, 128], F32, kind="ExternalOutput"),
                 "dbg_me", 128)
        P.dma("sync", dme.ap(), ME.c(0, 128), reads=[ME], writes=[dme])

    P.dma("sync", bass.AP(tensor=y_out.h, offset=0,
                          ap=[[T, 128], [128 * T, KD], [1, T]]),
          X.c(0, KD * T), reads=[X], writes=[y_out])
    P.barrier_wait("sync", [y_out])

    P.finalize()
    return P.nc


def _prep_host(inputs):
    f32 = np.float32

    def cnp(a):
        return np.ascontiguousarray(np.asarray(a, dtype=f32))

    ins = {}
    ins["wrt"] = cnp(inputs["L0_attn_Wr"].T)
    ins["wkt"] = cnp(inputs["L0_attn_Wk"].T)
    ins["wvt"] = cnp(inputs["L0_attn_Wv"].T)
    ins["wot"] = cnp(inputs["L0_attn_Wo"].T)
    ins["w1t"] = cnp(inputs["L0_attn_W1"].T)
    ins["w2t"] = cnp(inputs["L0_attn_W2"].T)
    ins["mw1t"] = cnp(np.transpose(inputs["L0_ffn_w1"], (0, 2, 1)).reshape(NE * D, FF))
    ins["mw2t"] = cnp(np.transpose(inputs["L0_ffn_w2"], (0, 2, 1)).reshape(NE * FF, D))
    ins["b2moe"] = cnp(inputs["L0_ffn_b2"])
    ins["w_in"] = cnp(inputs["L1_attn_W_in"])
    ins["w_dt"] = cnp(inputs["L1_attn_W_dt"])
    ins["w_out"] = cnp(inputs["L1_attn_W_out"])
    ins["wg_raw"] = cnp(inputs["L1_ffn_w_gate"])
    ins["wu_raw"] = cnp(inputs["L1_ffn_w_up"])
    ins["wd_raw"] = cnp(inputs["L1_ffn_w_down"])
    ins["clips"] = cnp(np.stack([inputs["L1_ffn_clip_x"][0],
                                 inputs["L1_ffn_clip_down"][0]])[None, :])

    def fmv(a):
        a = np.asarray(a, dtype=f32)
        n = a.shape[0] // 128
        return a.reshape(n, 128).T

    cv = np.zeros((128, CV_W), dtype=f32)
    cv[:, CO["ones"]] = 1.0
    cv[:, CO["dec"]:CO["dec"] + TC] = 0.9
    cv[:, CO["ln1g"]:CO["ln1g"] + KD] = fmv(inputs["L0_attn_ln1_g"])
    cv[:, CO["ln1b"]:CO["ln1b"] + KD] = fmv(inputs["L0_attn_ln1_b"])
    cv[:, CO["ln2g"]:CO["ln2g"] + KD] = fmv(inputs["L0_attn_ln2_g"])
    cv[:, CO["ln2b"]:CO["ln2b"] + KD] = fmv(inputs["L0_attn_ln2_b"])
    cv[:, CO["b1f"]:CO["b1f"] + KF] = fmv(inputs["L0_attn_b1"])
    cv[:, CO["b2f"]:CO["b2f"] + KD] = fmv(inputs["L0_attn_b2"])
    mb1 = np.asarray(inputs["L0_ffn_b1"], dtype=f32)
    cv[:, CO["moeb1"]:CO["moeb1"] + NE * KF] = np.concatenate(
        [fmv(mb1[e]) for e in range(NE)], axis=1)
    cv[:, CO["bin"]:CO["bin"] + 16] = fmv(inputs["L1_attn_b_in"])
    cw = np.asarray(inputs["L1_attn_conv_w"], dtype=f32).reshape(D, 4)
    cv[:, CO["convw"]:CO["convw"] + KD * 4] = cw.reshape(KD, 128, 4).transpose(
        1, 0, 2).reshape(128, KD * 4)
    cv[:, CO["convb"]:CO["convb"] + KD] = fmv(inputs["L1_attn_conv_b"])
    cv[:, CO["bdt"]:CO["bdt"] + KD] = fmv(inputs["L1_attn_b_dt"])
    al = np.asarray(inputs["L1_attn_A_log"], dtype=f32)
    cv[:, CO["Aneg"]:CO["Aneg"] + KD * NST] = al.reshape(KD, 128, NST).transpose(
        1, 0, 2).reshape(128, KD * NST)
    cv[:, CO["Dv"]:CO["Dv"] + KD] = fmv(inputs["L1_attn_D"])
    cv[:, CO["bout"]:CO["bout"] + KD] = fmv(inputs["L1_attn_b_out"])
    cv[:, CO["lngm"]:CO["lngm"] + KD] = fmv(inputs["L1_attn_ln_g"])
    cv[:, CO["lnbm"]:CO["lnbm"] + KD] = fmv(inputs["L1_attn_ln_b"])
    wbc = np.asarray(inputs["L1_attn_W_bc"], dtype=f32)
    cv[:, CO["wbc"]:CO["wbc"] + KD * 32] = wbc.reshape(KD, 128, 32).transpose(
        1, 0, 2).reshape(128, KD * 32)
    rw = np.asarray(inputs["L0_ffn_router_w"], dtype=f32)
    rwt = rw.T
    cv[:, CO["rwt"]:CO["rwt"] + KD * NE] = rwt.reshape(KD, 128, NE).transpose(
        1, 0, 2).reshape(128, KD * NE)
    bbc = np.asarray(inputs["L1_attn_b_bc"], dtype=f32)
    cv[:32, CO["bbc"]] = bbc
    ins["cvec"] = cnp(cv)
    return ins


def kernel(**inputs):
    shared = _prep_host(inputs)
    x = np.asarray(inputs["L0_attn_x"], dtype=np.float32)
    in_maps = []
    for c in range(NCORE):
        m = dict(shared)
        m["x_fm"] = np.ascontiguousarray(x[c].T)
        in_maps.append(m)
    nc = build_program()
    res = run_bass_kernel_spmd(nc, in_maps, core_ids=list(range(NCORE)))
    out = np.stack([np.asarray(res.results[c]["y_fm"]).T for c in range(NCORE)])
    return out.astype(np.float32)


# version 11
# speedup vs baseline: 1.0159x; 1.0159x over previous
import numpy as np
import concourse.bass as bass
import concourse.mybir as mybir
from concourse.bass_utils import run_bass_kernel_spmd
from concourse.alu_op_type import AluOpType as OP

F32 = mybir.dt.float32
I32 = mybir.dt.int32
U32 = mybir.dt.uint32
AF = mybir.ActivationFunctionType
AXX = mybir.AxisListType.X

NCORE = 8
T = 2048
D = 1024
KD = 8
FF = 4096
KF = 32
NE = 8
FI = 2816
KI = 22
NST = 16
TC = 256
NCH = T // TC
LM = 128
TBK = 512
EPS = 1e-5
RK = 640.0
R127 = 1.0 / 127.0

ENGS = ("sync", "tensor", "vector", "scalar", "gpsimd")
DEBUG = False
STOP_AFTER = None

_off = 0
def _nxt(n):
    global _off
    o = _off
    _off += n
    return o
CO = {}
CO["ones"] = _nxt(1)
CO["dec"] = _nxt(TC)
CO["ln1g"] = _nxt(KD)
CO["ln1b"] = _nxt(KD)
CO["ln2g"] = _nxt(KD)
CO["ln2b"] = _nxt(KD)
CO["b1f"] = _nxt(KF)
CO["hb1"] = _nxt(KF)
CO["b2f"] = _nxt(KD)
CO["hb2"] = _nxt(KD)
CO["moeb1"] = _nxt(NE * KF)
CO["bin"] = _nxt(16)
CO["convw"] = _nxt(KD * 4)
CO["convb"] = _nxt(KD)
CO["bdt"] = _nxt(KD)
CO["Aneg"] = _nxt(KD * NST)
CO["Dv"] = _nxt(KD)
CO["bout"] = _nxt(KD)
CO["lngm"] = _nxt(KD)
CO["lnbm"] = _nxt(KD)
CO["wbc"] = _nxt(KD * 32)
CO["rwt"] = _nxt(KD * NE)
CO["bbc"] = _nxt(1)
CV_W = _off + (0 if _off % 2 == 0 else 1)


class TB:
    __slots__ = ("h", "name", "fsz")

    def __init__(self, h, name, fsz):
        self.h = h
        self.name = name
        self.fsz = fsz

    def ap(self):
        return self.h.ap()

    def sl(self, p0, np_, off, dims):
        return bass.AP(tensor=self.h, offset=p0 * self.fsz + off,
                       ap=[[self.fsz, np_]] + [list(d) for d in dims])

    def c(self, off, n, p0=0, np_=128):
        return self.sl(p0, np_, off, [[1, n]])


class Prog:
    def __init__(self):
        self.nc = bass.Bass("TRN2", target_bir_lowering=False, debug=False)
        self.q = {e: [] for e in ENGS}
        self.cnt = {e: 0 for e in ENGS}
        self.ndma = {e: 0 for e in ENGS}
        self.dcnt = {(e, j): 0 for e in ENGS for j in range(8)}
        self.seen = {e: {} for e in ENGS}
        self.lastw = {}
        self.readers = {}
        self.nops = 0

    def _need(self, e, key, val):
        if key[0] == "dma":
            val = self.dcnt[(key[1], key[2])]
        if self.seen[e].get(key, 0) >= val:
            return
        self.q[e].append(("w", key, val))
        self.seen[e][key] = val

    def _dep(self, e, reads, writes):
        for r in reads:
            lw = self.lastw.get(id(r))
            if lw is not None:
                self._need(e, lw[0], lw[1])
        for w in writes:
            lw = self.lastw.get(id(w))
            if lw is not None:
                self._need(e, lw[0], lw[1])
            for key, val in self.readers.get(id(w), {}).items():
                self._need(e, key, val)

    def _mark(self, key, val, reads, writes):
        for w in writes:
            self.lastw[id(w)] = (key, val)
            self.readers[id(w)] = {}
        for r in reads:
            d = self.readers.setdefault(id(r), {})
            if d.get(key, 0) < val:
                d[key] = val

    def op(self, e, fn, reads=(), writes=()):
        self._dep(e, reads, writes)
        self.cnt[e] += 1
        key, val = ("ord", e), self.cnt[e]
        self.q[e].append(("i", fn, [(key, 1)]))
        self._mark(key, val, reads, writes)
        self.nops += 1

    def dma(self, e, out_ap, in_ap, reads=(), writes=(), accum=False):
        self._dep(e, reads, writes)
        j = self.ndma[e] % 8
        self.ndma[e] += 1
        self.dcnt[(e, j)] += 16
        key, val = ("dma", e, j), self.dcnt[(e, j)]
        if accum:
            def fn(eng, o=out_ap, i=in_ap):
                return eng.dma_start(out=o, in_=i, accum_op=OP.add)
        else:
            def fn(eng, o=out_ap, i=in_ap):
                return eng.dma_start(out=o, in_=i)
        self.q[e].append(("i", fn, [(key, 16)]))
        self._mark(key, val, reads, writes)
        self.nops += 1

    def idma(self, out_ap, out_off, in_ap, in_off, reads=(), writes=(), accum=False):
        e = "gpsimd"
        self._dep(e, reads, writes)
        j = self.ndma[e] % 8
        self.ndma[e] += 1
        self.dcnt[(e, j)] += 16
        key, val = ("dma", e, j), self.dcnt[(e, j)]
        op = OP.add if accum else OP.bypass
        def fn(eng, o=out_ap, oo=out_off, i=in_ap, io=in_off, cop=op):
            return eng.indirect_dma_start(out=o, out_offset=oo, in_=i,
                                          in_offset=io, compute_op=cop)
        self.q[e].append(("i", fn, [(key, 16)]))
        self._mark(key, val, reads, writes)
        self.nops += 1

    def barrier_wait(self, e, tbs):
        self._dep(e, tbs, ())

    def merge_alias(self, dst, srcs):
        for src in srcs:
            lw = self.lastw.get(id(src))
            if lw is not None:
                d = self.readers.setdefault(id(dst), {})
                if d.get(lw[0], 0) < lw[1]:
                    d[lw[0]] = lw[1]
            for key, val in self.readers.get(id(src), {}).items():
                d = self.readers.setdefault(id(dst), {})
                if d.get(key, 0) < val:
                    d[key] = val

    def finalize(self):
        import contextlib
        nc = self.nc
        with contextlib.ExitStack() as st:
            sems = {}
            for e in ENGS:
                sems[("ord", e)] = st.enter_context(nc.semaphore(f"o_{e}"))
                if self.ndma[e]:
                    for j in range(8):
                        sems[("dma", e, j)] = st.enter_context(
                            nc.semaphore(f"d_{e}{j}"))
            blk = st.enter_context(nc.Block())
            for e in ENGS:
                items = self.q[e]
                if not items:
                    continue

                def body(eng, items=items, sems=sems):
                    for it in items:
                        if it[0] == "w":
                            eng.wait_ge(sems[it[1]], it[2])
                        else:
                            inst = it[1](eng)
                            for key, d in it[2]:
                                inst.then_inc(sems[key], d)
                getattr(blk, e)(body)


def build_program():
    P = Prog()
    nc = P.nc

    def dram_in(name, shape):
        return TB(nc.dram_tensor(name, list(shape), F32, kind="ExternalInput"),
                  name, int(np.prod(shape[1:])))

    def dram_scr(name, shape):
        return TB(nc.dram_tensor(name, list(shape), F32, kind="Internal"),
                  name, int(np.prod(shape[1:])))

    def sbuf(name, fsz, dtype=F32):
        return TB(nc.alloc_sbuf_tensor(name, [128, fsz], dtype), name, fsz)

    def psum(name, fsz):
        return TB(nc.alloc_psum_tensor(name, [128, fsz], F32), name, fsz)

    x_in = dram_in("x_fm", (D, T))
    wrt = dram_in("wrt", (D, D))
    wkt = dram_in("wkt", (D, D))
    wvt = dram_in("wvt", (D, D))
    wot = dram_in("wot", (D, D))
    w1t = dram_in("w1t", (D, FF))
    w2t = dram_in("w2t", (FF, D))
    mw1t = dram_in("mw1t", (NE * D, FF))
    mw2t = dram_in("mw2t", (NE * FF, D))
    b2moe = dram_in("b2moe", (NE, D))
    w_in = dram_in("w_in", (D, 2 * D))
    w_dt = dram_in("w_dt", (D, D))
    w_out = dram_in("w_out", (D, D))
    wg_raw = dram_in("wg_raw", (FI, D))
    wu_raw = dram_in("wu_raw", (FI, D))
    wd_raw = dram_in("wd_raw", (D, FI))
    cvec = dram_in("cvec", (128, CV_W))
    clips = dram_in("clips", (1, 2))

    y_out = TB(nc.dram_tensor("y_fm", [D, T], F32, kind="ExternalOutput"),
               "y_fm", T)

    xn0_d = dram_scr("xn0_d", (D, T))
    xn1_d = dram_scr("xn1_d", (D, T))
    moe_d = dram_scr("moe_d", (T + 4, D))
    xg_d = dram_scr("xg_d", (D, T))
    dt_d = dram_scr("dt_d", (D, T))
    ys_d = dram_scr("ys_d", (D, T))
    bc_d = dram_scr("bc_d", (32, T))
    h_d = dram_scr("h_d", (FI, T))
    hq_d = dram_scr("hq_d", (FI, T))
    xn_tm = dram_scr("xn_tm", (T + 1, D))
    list_d = dram_scr("list_d", (NE, 2176, 2))
    cc_in1 = dram_scr("cc_in1", (1, 4))
    cc_out1 = dram_scr("cc_out1", (1, 4))
    cc_in2 = dram_scr("cc_in2", (1, 4))
    cc_out2 = dram_scr("cc_out2", (1, 4))

    X = sbuf("X", KD * T)
    A8a = sbuf("A8a", 8192)
    A8b = sbuf("A8b", 8192)
    XN2 = sbuf("XN2", 2048)
    GTMP = sbuf("GTMP", 256)
    A2 = [sbuf(f"A2_{i}", 2048) for i in range(4)]
    WR1 = sbuf("WR1", 2048)
    WR2 = sbuf("WR2", 2048)
    CV = sbuf("CV", CV_W)
    SM = sbuf("SM", 1024)
    SM2 = sbuf("SM2", 1024)
    MRE = sbuf("MRE", 512)
    RRE = sbuf("RRE", 512)
    STV = sbuf("STV", 128)
    MST = sbuf("MST", 128)
    ME = sbuf("ME", 128)
    LG = sbuf("LG", 128)
    IDXU = sbuf("IDXU", 8, U32)
    SID = sbuf("SID", 128)
    WBCs = sbuf("WBCs", 256)
    TRI = sbuf("TRI", 128)
    TOKI = sbuf("TOKI", 16)
    GSC = sbuf("GSC", 16)
    SXR = sbuf("SXR", 4)
    ITI = sbuf("ITI", 512, I32)

    PS0 = psum("PS0", 512)
    PS1 = psum("PS1", 512)
    PS2 = psum("PS2", 512)
    PS3 = psum("PS3", 512)
    PSW0 = psum("PSW0", 1024)
    PSW1 = psum("PSW1", 1024)

    def mm(pst, out_ap, lhsT_ap, rhs_ap, first, last, reads):
        def fn(eng, o=out_ap, l=lhsT_ap, r=rhs_ap, f=first, s=last):
            return eng.matmul(o, l, r, start=f, stop=s)
        P.op("tensor", fn, reads=reads, writes=[pst])

    def tpose(pst, out_ap, in_ap, reads):
        def fn(eng, o=out_ap, i=in_ap, idm=SID.c(0, 128)):
            return eng.transpose(o, i, idm)
        P.op("tensor", fn, reads=list(reads) + [SID], writes=[pst])

    def act(out_ap, in_ap, func, reads, writes, scale=1.0, bias=0.0):
        def fn(eng, o=out_ap, i=in_ap, f=func, s=scale, b=bias):
            return eng.activation(out=o, in_=i, func=f, bias=b, scale=s)
        P.op("scalar", fn, reads=reads, writes=writes)

    def v_tt(out_ap, a_ap, b_ap, opk, reads, writes):
        def fn(e, o=out_ap, a=a_ap, b=b_ap, k=opk):
            return e.tensor_tensor(out=o, in0=a, in1=b, op=k)
        P.op("vector", fn, reads=reads, writes=writes)

    def v_ts(out_ap, in_ap, s1, o1, s2, o2, reads, writes):
        def fn(e, o=out_ap, i=in_ap, a=s1, ka=o1, b=s2, kb=o2):
            return e.tensor_scalar(out=o, in0=i, scalar1=a, scalar2=b, op0=ka, op1=kb)
        P.op("vector", fn, reads=reads, writes=writes)

    def v_ts1(out_ap, in_ap, s1, o1, reads, writes):
        def fn(e, o=out_ap, i=in_ap, a=s1, ka=o1):
            return e.tensor_single_scalar(out=o, in_=i, scalar=a, op=ka)
        P.op("vector", fn, reads=reads, writes=writes)

    def v_stt(out_ap, in0_ap, scal, in1_ap, o0, o1, reads, writes):
        def fn(e, o=out_ap, i0=in0_ap, s=scal, i1=in1_ap, k0=o0, k1=o1):
            return e.scalar_tensor_tensor(out=o, in0=i0, scalar=s, in1=i1,
                                          op0=k0, op1=k1)
        P.op("vector", fn, reads=reads, writes=writes)

    def v_copy(out_ap, in_ap, reads, writes):
        def fn(e, o=out_ap, i=in_ap):
            return e.tensor_copy(out=o, in_=i)
        P.op("vector", fn, reads=reads, writes=writes)

    def v_red(out_ap, in_ap, opk, reads, writes, absv=False):
        def fn(e, o=out_ap, i=in_ap, k=opk, a=absv):
            return e.tensor_reduce(out=o, in_=i, axis=AXX, op=k,
                                   apply_absolute_value=a)
        P.op("vector", fn, reads=reads, writes=writes)

    def v_recip(out_ap, in_ap, reads, writes):
        def fn(e, o=out_ap, i=in_ap):
            return e.reciprocal(out=o, in_=i)
        P.op("vector", fn, reads=reads, writes=writes)

    def memset(tb, val, off=0, n=None, eng="vector"):
        n = tb.fsz if n is None else n
        def fn(e, a=tb.c(off, n), v=val):
            return e.memset(a, v)
        P.op(eng, fn, reads=(), writes=[tb])

    def load_group(ring, slot, src, rows0, nrows_k, col0, width=128):
        ap = bass.AP(tensor=src.h, offset=rows0 * src.fsz + col0,
                     ap=[[src.fsz, 128], [128 * src.fsz, nrows_k], [1, width]])
        P.dma("sync", ring.sl(0, 128, slot * 1024, [[width, nrows_k], [1, width]]),
              ap, reads=[src], writes=[ring])

    def rsl(ring, slot, j, width=128):
        return ring.sl(0, 128, slot * 1024 + j * width, [[1, width]])

    def fm_dram(dst, k, t0, n):
        return bass.AP(tensor=dst.h, offset=k * 128 * T + t0,
                       ap=[[T, 128], [1, n]])

    def ln_stats(chunk_ap_fn, tlen, reads_of):
        for k in range(KD):
            src_ap, src_tb = chunk_ap_fn(k)
            mm(PS2, PS2.c(0, tlen, 0, 1), CV.c(CO["ones"], 1), src_ap,
               k == 0, k == KD - 1, reads=[src_tb, CV])
            act(SM2.c(0, tlen), src_ap, AF.Square, reads=[src_tb], writes=[SM2])
            mm(PS3, PS3.c(0, tlen, 0, 1), CV.c(CO["ones"], 1), SM2.c(0, tlen),
               k == 0, k == KD - 1, reads=[SM2, CV])
        v_ts1(SM2.c(0, tlen, 0, 1), PS2.c(0, tlen, 0, 1), 1.0 / D, OP.mult,
              reads=[PS2], writes=[SM2])
        v_ts1(SM2.c(512, tlen, 0, 1), PS3.c(0, tlen, 0, 1), 1.0 / D, OP.mult,
              reads=[PS3], writes=[SM2])
        v_tt(SM.c(0, tlen, 0, 1), SM2.c(0, tlen, 0, 1), SM2.c(0, tlen, 0, 1),
             OP.mult, reads=[SM2], writes=[SM])
        v_tt(SM.c(0, tlen, 0, 1), SM2.c(512, tlen, 0, 1), SM.c(0, tlen, 0, 1),
             OP.subtract, reads=[SM2, SM], writes=[SM])
        v_ts1(SM.c(0, tlen, 0, 1), SM.c(0, tlen, 0, 1), EPS, OP.add,
              reads=[SM], writes=[SM])
        v_recip(SM.c(0, tlen, 0, 1), SM.c(0, tlen, 0, 1), reads=[SM], writes=[SM])
        act(SM.c(0, tlen, 0, 1), SM.c(0, tlen, 0, 1), AF.Sqrt, reads=[SM],
            writes=[SM])
        P.dma("sync", MRE.c(0, tlen), SM2.sl(0, 1, 0, [[0, 128], [1, tlen]]),
              reads=[SM2], writes=[MRE])
        P.dma("sync", RRE.c(0, tlen), SM.sl(0, 1, 0, [[0, 128], [1, tlen]]),
              reads=[SM], writes=[RRE])

    def ln_apply(src_ap, src_tb, out_ap, out_tb, tlen, gcol=None, bcol=None):
        v_tt(SM2.c(0, tlen), src_ap, MRE.c(0, tlen), OP.subtract,
             reads=[src_tb, MRE], writes=[SM2])
        if gcol is None:
            v_tt(out_ap, SM2.c(0, tlen), RRE.c(0, tlen), OP.mult,
                 reads=[SM2, RRE], writes=[out_tb])
        else:
            v_tt(SM2.c(0, tlen), SM2.c(0, tlen), RRE.c(0, tlen), OP.mult,
                 reads=[SM2, RRE], writes=[SM2])
            v_ts(out_ap, SM2.c(0, tlen), gcol, OP.mult, bcol, OP.add,
                 reads=[SM2, CV], writes=[out_tb])

    def quant_tile(dst_ap, dst_tb, src_ap, src_tb, inv_col, inv_tb, n, clip=True,
                   scratch=None):
        sc = scratch
        v_ts(sc.c(0, n), src_ap, inv_col, OP.mult, RK, OP.add,
             reads=[src_tb, inv_tb], writes=[sc])
        v_copy(ITI.c(0, n), sc.c(0, n), reads=[sc], writes=[ITI])
        v_copy(sc.c(n, n), ITI.c(0, n), reads=[ITI], writes=[sc])
        v_tt(sc.c(0, n), sc.c(0, n), sc.c(n, n), OP.subtract, reads=[sc],
             writes=[sc])
        v_stt(sc.c(0, n), sc.c(0, n), 0.5, sc.c(n, n), OP.is_ge, OP.add,
              reads=[sc], writes=[sc])
        if clip:
            v_ts(sc.c(0, n), sc.c(0, n), RK - 128.0, OP.max, RK + 127.0, OP.min,
                 reads=[sc], writes=[sc])
        v_ts1(dst_ap, sc.c(0, n), RK, OP.subtract, reads=[sc], writes=[dst_tb])

    memset(SID, 0.0, eng="gpsimd")
    def fn_id(e, a=SID.c(0, 128)):
        return e.affine_select(out=a, in_=a, compare_op=OP.not_equal, fill=1.0,
                               base=0, pattern=[[-1, 128]], channel_multiplier=1)
    P.op("gpsimd", fn_id, reads=[SID], writes=[SID])

    memset(TRI, 0.0, eng="gpsimd")
    def fn_tri(e, a=TRI.c(0, 128)):
        return e.affine_select(out=a, in_=a, compare_op=OP.is_ge, fill=1.0,
                               base=0, pattern=[[-1, 128]], channel_multiplier=1)
    P.op("gpsimd", fn_tri, reads=[TRI], writes=[TRI])
    def fn_toki(e, a=TOKI.c(0, 16)):
        return e.iota(a, pattern=[[128, 16]], base=0, channel_multiplier=1,
                      allow_small_or_imprecise_dtypes=True)
    P.op("gpsimd", fn_toki, reads=(), writes=[TOKI])
    P.dma("sync", CV.c(0, CV_W), cvec.c(0, CV_W), reads=[cvec], writes=[CV])
    P.dma("sync", X.c(0, KD * T),
          bass.AP(tensor=x_in.h, offset=0, ap=[[T, 128], [128 * T, KD], [1, T]]),
          reads=[x_in], writes=[X])
    P.dma("sync", GSC.c(0, 2, 0, 1), clips.c(0, 2, 0, 1), reads=[clips],
          writes=[GSC])
    v_copy(WBCs.c(0, 256), CV.c(CO["wbc"], 256), reads=[CV], writes=[WBCs])
    v_ts1(CV.c(CO["hb1"], KF), CV.c(CO["b1f"], KF), 0.5, OP.mult, reads=[CV],
          writes=[CV])
    v_ts1(CV.c(CO["hb2"], KD), CV.c(CO["b2f"], KD), 0.5, OP.mult, reads=[CV],
          writes=[CV])
    act(CV.c(CO["Aneg"], KD * NST), CV.c(CO["Aneg"], KD * NST), AF.Exp,
        reads=[CV], writes=[CV])
    v_ts1(CV.c(CO["Aneg"], KD * NST), CV.c(CO["Aneg"], KD * NST), -1.0, OP.mult,
          reads=[CV], writes=[CV])

    XNCb = [A2[0], XN2]
    X1C, KVB, HOB = A2[1], A2[2], A2[3]
    V3_O, V1_O, V2_O, H8_O = 0, 32, 64, 72
    RKV0 = 0
    H2U0 = 24 * TC
    AO_OFF = 0
    memset(STV, 0.0)
    PSR = [PS0, PS1, PS2, PS3]

    def lif_steps(ubuf, ubase, ngr, tc, vs_off):
        for t in range(tc):
            u_ap = ubuf.c(ubase + t * ngr, ngr)
            vs_ap = STV.c(vs_off, ngr)
            v_stt(u_ap, vs_ap, 0.5, u_ap, OP.mult, OP.add,
                  reads=[STV, ubuf], writes=[ubuf])
            v_stt(vs_ap, u_ap, 1.0, u_ap, OP.is_lt, OP.mult,
                  reads=[ubuf], writes=[STV])

    def g_tt(out_ap, a_ap, b_ap, opk, reads, writes):
        def fn(e, o=out_ap, a=a_ap, b=b_ap, k=opk):
            return e.tensor_tensor(out=o, in0=a, in1=b, op=k)
        P.op("gpsimd", fn, reads=reads, writes=writes)

    def g_ts(out_ap, in_ap, s1, o1, s2, o2, reads, writes):
        def fn(e, o=out_ap, i=in_ap, a=s1, ka=o1, b=s2, kb=o2):
            return e.tensor_scalar(out=o, in0=i, scalar1=a, scalar2=b, op0=ka, op1=kb)
        P.op("gpsimd", fn, reads=reads, writes=writes)

    def ln_apply_g(src_ap, src_tb, out_ap, out_tb, tlen, gcol=None, bcol=None):
        g_tt(GTMP.c(0, tlen), src_ap, MRE.c(0, tlen), OP.subtract,
             [src_tb, MRE], [GTMP])
        if gcol is None:
            g_tt(out_ap, GTMP.c(0, tlen), RRE.c(0, tlen), OP.mult,
                 [GTMP, RRE], [out_tb])
        else:
            g_tt(GTMP.c(0, tlen), GTMP.c(0, tlen), RRE.c(0, tlen), OP.mult,
                 [GTMP, RRE], [GTMP])
            g_ts(out_ap, GTMP.c(0, tlen), gcol, OP.mult, bcol, OP.add,
                 [GTMP, CV], [out_tb])

    def ln0_chunk(i):
        t0 = i * TC
        XNC = XNCb[i % 2]
        ln_stats(lambda k: (X.c(k * T + t0, TC), X), TC, X)
        for k in range(KD):
            ln_apply_g(X.c(k * T + t0, TC), X, XNC.c(k * TC, TC), XNC, TC)
        P.dma("sync", bass.AP(tensor=xn0_d.h, offset=t0,
                              ap=[[T, 128], [128 * T, KD], [1, TC]]),
              XNC.sl(0, 128, 0, [[TC, KD], [1, TC]]), reads=[XNC], writes=[xn0_d])
        for tt in range(TC // 128):
            for k in range(KD):
                mm(PSW0, PSW0.c(0, 8), XNC.c(k * TC + tt * 128, 128),
                   CV.c(CO["rwt"] + k * 8, 8), k == 0, k == KD - 1, reads=[XNC, CV])
            v_copy(LG.c((i * 2 + tt) * 8, 8), PSW0.c(0, 8), reads=[PSW0],
                   writes=[LG])

    def rkv_chunk(i):
        XNC = XNCb[i % 2]
        for si, wsrc in enumerate((wrt, wkt, wvt)):
            for ko in range(KD):
                g = si * KD + ko
                slot = g % 2
                load_group(WR1, slot, wsrc, 0, KD, ko * 128)
                pst = PSR[g % 4]
                for kin in range(KD):
                    mm(pst, pst.c(0, TC), rsl(WR1, slot, kin),
                       XNC.c(kin * TC, TC), kin == 0, kin == KD - 1,
                       reads=[WR1, XNC])
                act(A8a.sl(0, 128, RKV0 + g, [[24, TC]]), pst.c(0, TC),
                    AF.Identity, scale=0.5, reads=[pst], writes=[A8a])

    ln0_chunk(0)
    rkv_chunk(0)
    for i in range(NCH):
        t0 = i * TC
        XNC = XNCb[i % 2]
        if i + 1 < NCH:
            ln0_chunk(i + 1)
        lif_steps(A8a, RKV0, 24, TC, V3_O)
        v_ts1(A8a.c(RKV0, 24 * TC), A8a.c(RKV0, 24 * TC), 1.0, OP.is_ge,
              reads=[A8a], writes=[A8a])
        v_tt(KVB.sl(0, 128, 0, [[1, TC], [TC, KD]]),
             A8a.sl(0, 128, RKV0 + 8, [[24, TC], [1, KD]]),
             A8a.sl(0, 128, RKV0 + 16, [[24, TC], [1, KD]]), OP.mult,
             reads=[A8a], writes=[KVB])
        for j in range(KD):
            def fnsc(e, o=HOB.c(j * TC, TC), d0=CV.c(CO["dec"], TC),
                     d1=KVB.c(j * TC, TC), ini=STV.c(H8_O + j, 1)):
                return e.tensor_tensor_scan(out=o, data0=d0, data1=d1,
                                            initial=ini, op0=OP.mult, op1=OP.add)
            P.op("vector", fnsc, reads=[CV, KVB, STV], writes=[HOB])
            v_copy(STV.c(H8_O + j, 1), HOB.c(j * TC + TC - 1, 1), reads=[HOB],
                   writes=[STV])
        v_tt(KVB.sl(0, 128, 0, [[1, TC], [TC, KD]]),
             A8a.sl(0, 128, RKV0, [[24, TC], [1, KD]]),
             HOB.sl(0, 128, 0, [[1, TC], [TC, KD]]), OP.mult,
             reads=[A8a, HOB], writes=[KVB])
        if i + 1 < NCH:
            rkv_chunk(i + 1)
        for ko in range(KD):
            slot = ko % 2
            load_group(WR1, slot, wot, 0, KD, ko * 128)
            pst = PS0 if slot == 0 else PS1
            for kin in range(KD):
                mm(pst, pst.c(0, TC), rsl(WR1, slot, kin), KVB.c(kin * TC, TC),
                   kin == 0, kin == KD - 1, reads=[WR1, KVB])
            act(A8b.c(AO_OFF + ko * TC, TC), pst.c(0, TC), AF.Identity,
                reads=[pst], writes=[A8b])
        g_tt(A8b.c(AO_OFF, KD * TC), XNC.c(0, KD * TC), A8b.c(AO_OFF, KD * TC),
             OP.add, [XNC, A8b], [A8b])
        ln_stats(lambda k: (A8b.c(AO_OFF + k * TC, TC), A8b), TC, A8b)
        for k in range(KD):
            ln_apply_g(A8b.c(AO_OFF + k * TC, TC), A8b, X1C.c(k * TC, TC), X1C,
                       TC, gcol=CV.c(CO["ln1g"] + k, 1),
                       bcol=CV.c(CO["ln1b"] + k, 1))
        for fo in range(KF):
            slot = fo % 2
            load_group(WR1, slot, w1t, 0, KD, fo * 128)
            pst = PS0 if slot == 0 else PS1
            for kin in range(KD):
                mm(pst, pst.c(0, TC), rsl(WR1, slot, kin), X1C.c(kin * TC, TC),
                   kin == 0, kin == KD - 1, reads=[WR1, X1C])
            act(A8b.sl(0, 128, fo, [[KF, TC]]), pst.c(0, TC), AF.Identity,
                scale=0.5, bias=CV.c(CO["hb1"] + fo, 1), reads=[pst, CV],
                writes=[A8b])
        lif_steps(A8b, 0, KF, TC, V1_O)
        for gi in range(4):
            sbuf = KVB if gi % 2 == 0 else HOB
            v_ts1(sbuf.sl(0, 128, 0, [[1, TC], [TC, 8]]),
                  A8b.sl(0, 128, gi * 8, [[KF, TC], [1, 8]]), 1.0, OP.is_ge,
                  reads=[A8b], writes=[sbuf])
            for ko in range(KD):
                slot = (gi * KD + ko) % 2
                load_group(WR2, slot, w2t, gi * 8 * 128, 8, ko * 128)
                pst = PSR[ko // 2]
                pap = pst.c((ko % 2) * TC, TC)
                for kin in range(8):
                    mm(pst, pap, rsl(WR2, slot, kin), sbuf.c(kin * TC, TC),
                       gi == 0 and kin == 0, gi == 3 and kin == 7,
                       reads=[WR2, sbuf])
        for ko in range(KD):
            pst = PSR[ko // 2]
            act(A8a.sl(0, 128, H2U0 + ko, [[KD, TC]]), pst.c((ko % 2) * TC, TC),
                AF.Identity, scale=0.5, bias=CV.c(CO["hb2"] + ko, 1),
                reads=[pst, CV], writes=[A8a])
        lif_steps(A8a, H2U0, KD, TC, V2_O)
        v_ts1(KVB.sl(0, 128, 0, [[1, TC], [TC, KD]]),
              A8a.sl(0, 128, H2U0, [[KD, TC], [1, KD]]), 1.0, OP.is_ge,
              reads=[A8a], writes=[KVB])
        g_tt(KVB.c(0, KD * TC), X1C.c(0, KD * TC), KVB.c(0, KD * TC),
             OP.add, [X1C, KVB], [KVB])
        ln_stats(lambda k: (KVB.c(k * TC, TC), KVB), TC, KVB)
        for k in range(KD):
            ln_apply_g(KVB.c(k * TC, TC), KVB, A8a.c(H2U0 + k * TC, TC), A8a, TC,
                       gcol=CV.c(CO["ln2g"] + k, 1), bcol=CV.c(CO["ln2b"] + k, 1))
            g_tt(X.c(k * T + t0, TC), X.c(k * T + t0, TC),
                 A8a.c(H2U0 + k * TC, TC), OP.add, [X, A8a], [X])

    if DEBUG:
        dbg_xB = TB(nc.dram_tensor("dbg_xB", [D, T], F32, kind="ExternalOutput"),
                    "dbg_xB", T)
        P.dma("sync", bass.AP(tensor=dbg_xB.h, offset=0,
                              ap=[[T, 128], [128 * T, KD], [1, T]]),
              X.c(0, KD * T), reads=[X], writes=[dbg_xB])

    if STOP_AFTER == "B0":
        P.dma("sync", bass.AP(tensor=y_out.h, offset=0,
                              ap=[[T, 128], [128 * T, KD], [1, T]]),
              X.c(0, KD * T), reads=[X], writes=[y_out])
        P.barrier_wait("sync", [y_out])
        P.finalize()
        return P.nc

    B2R = SM2
    P.dma("sync", B2R.c(0, D, 0, NE), b2moe.c(0, D, 0, NE), reads=[b2moe],
          writes=[B2R])
    for tt in range(T // 128):
        def fnm(e, o=SM.c(0, 8), i=LG.c(tt * 8, 8)):
            return e.max(out=o, in_=i)
        P.op("vector", fnm, reads=[LG], writes=[SM])
        def fni(e, o=IDXU.c(0, 8), m=SM.c(0, 8), i=LG.c(tt * 8, 8)):
            return e.max_index(out=o, in_max=m, in_values=i)
        P.op("vector", fni, reads=[SM, LG], writes=[IDXU])
        v_copy(SM.c(16, 8), IDXU.c(0, 8), reads=[IDXU], writes=[SM])
        v_tt(SM.c(8, 1), SM.c(0, 1), SM.c(1, 1), OP.subtract, reads=[SM],
             writes=[SM])
        act(SM.c(9, 1), SM.c(8, 1), AF.Sigmoid, reads=[SM], writes=[SM])
        act(SM.c(10, 1), SM.c(8, 1), AF.Sigmoid, scale=-1.0, reads=[SM],
            writes=[SM])
        for e_ in range(NE):
            v_ts(SM.c(11, 1), SM.c(16, 1), float(e_), OP.is_equal, SM.c(9, 1),
                 OP.mult, reads=[SM], writes=[SM])
            v_ts(SM.c(12, 1), SM.c(17, 1), float(e_), OP.is_equal, SM.c(10, 1),
                 OP.mult, reads=[SM], writes=[SM])
            v_tt(ME.c(tt * 8 + e_, 1), SM.c(11, 1), SM.c(12, 1), OP.add,
                 reads=[SM], writes=[ME])
    for tt in range(T // 128):
        tpose(PS2, PS2.sl(0, 8, 0, [[1, 128]]), ME.sl(0, 128, tt * 8, [[1, 8]]),
              reads=[ME])
        v_copy(GTMP.sl(0, 8, 0, [[1, 128]]), PS2.sl(0, 8, 0, [[1, 128]]),
               reads=[PS2], writes=[GTMP])
        for hf in range(2):
            mm(PSW0, PSW0.c(hf * 512, 512), GTMP.sl(0, 8, 0, [[1, 128]]),
               B2R.c(hf * 512, 512, 0, NE), True, True, reads=[GTMP, B2R])
        v_copy(SM.c(0, 1024), PSW0.c(0, 1024), reads=[PSW0], writes=[SM])
        P.dma("sync", bass.AP(tensor=moe_d.h, offset=tt * 128 * D,
                              ap=[[D, 128], [1, D]]),
              SM.c(0, 1024), reads=[SM], writes=[moe_d])
    memset(GTMP, 0.0)
    memset(SM, 0.0)
    v_ts1(SM.c(0, 1), SM.c(1, 1), 2048.0, OP.add, reads=[SM], writes=[SM])
    for e_ in range(NE):
        P.dma("gpsimd", bass.AP(tensor=list_d.h, offset=e_ * 2176 * 2,
                                ap=[[2, 128], [256, 17], [1, 2]]),
              SM.sl(0, 128, 0, [[0, 17], [1, 2]]), reads=[SM], writes=[list_d])
    P.dma("sync", bass.AP(tensor=xn_tm.h, offset=T * D, ap=[[D, 1], [1, 1023]]),
          SM.c(1, 1023, 0, 1), reads=[SM], writes=[xn_tm])
    P.dma("sync", bass.AP(tensor=xn_tm.h, offset=T * D + 1023, ap=[[D, 1], [1, 1]]),
          SM.c(1, 1, 0, 1), reads=[SM], writes=[xn_tm])
    for tt in range(T // 128):
        for k in range(KD):
            P.dma("sync", A2[2].c(k * 128, 128),
                  fm_dram(xn0_d, k, tt * 128, 128), reads=[xn0_d], writes=[A2[2]])
        for k in range(KD):
            pst = PSR[k % 4]
            tpose(pst, pst.c(0, 128), A2[2].c(k * 128, 128), reads=[A2[2]])
            v_copy(A2[3].c(k * 128, 128), pst.c(0, 128), reads=[pst],
                   writes=[A2[3]])
        P.dma("sync", bass.AP(tensor=xn_tm.h, offset=tt * 128 * D,
                              ap=[[D, 128], [1, D]]),
              A2[3].c(0, 1024), reads=[A2[3]], writes=[xn_tm])
    for tt in range(T // 128):
        v_ts1(GTMP.c(0, 8), ME.c(tt * 8, 8), 0.0, OP.is_gt, reads=[ME],
              writes=[GTMP])
        mm(PSW1, PSW1.c(0, 8), TRI.c(0, 128), GTMP.c(0, 8), True, True,
           reads=[TRI, GTMP])
        mm(PSW0, PSW0.c(0, 8, 0, 1), CV.c(CO["ones"], 1), GTMP.c(0, 8),
           True, True, reads=[CV, GTMP])
        v_tt(GTMP.c(32, 8), PSW1.c(0, 8), GTMP.c(16, 8), OP.add,
             reads=[PSW1, GTMP], writes=[GTMP])
        v_tt(GTMP.c(40, 8), GTMP.c(32, 8), GTMP.c(0, 8), OP.mult,
             reads=[GTMP], writes=[GTMP])
        v_ts(GTMP.c(48, 8), GTMP.c(0, 8), -2050.0, OP.mult, 2050.0, OP.add,
             reads=[GTMP], writes=[GTMP])
        v_tt(GTMP.c(40, 8), GTMP.c(40, 8), GTMP.c(48, 8), OP.add,
             reads=[GTMP], writes=[GTMP])
        def fno(e, o=GTMP.c(24, 8, 0, 1), a=GTMP.c(24, 8, 0, 1),
                b=PSW0.c(0, 8, 0, 1)):
            return e.tensor_tensor(out=o, in0=a, in1=b, op=OP.add)
        P.op("vector", fno, reads=[GTMP, PSW0], writes=[GTMP])
        P.dma("sync", GTMP.c(16, 8),
              GTMP.sl(0, 1, 24, [[0, 128], [1, 8]]), reads=[GTMP], writes=[GTMP])
        for e_ in range(NE):
            v_copy(GTMP.c(56, 1), TOKI.c(tt, 1), reads=[TOKI], writes=[GTMP])
            v_copy(GTMP.c(57, 1), ME.c(tt * 8 + e_, 1), reads=[ME], writes=[GTMP])
            v_ts1(ITI.c(0, 1), GTMP.c(40 + e_, 1), float(e_ * 2176), OP.add,
                  reads=[GTMP], writes=[ITI])
            P.idma(bass.AP(tensor=list_d.h, offset=0, ap=[[2, 128], [1, 2]]),
                   bass.IndirectOffsetOnAxis(ITI.c(0, 1), 0),
                   GTMP.c(56, 2), None, reads=[GTMP, ITI], writes=[list_d])
    CAPB = (512, 256)
    CAP0 = (0, 512)
    for e_ in range(NE):
        for blk in range(2):
            nslot = CAPB[blk]
            s0 = CAP0[blk]
            ntt = nslot // 128
            for q in range(ntt):
                P.dma("sync", A2[3].c(q * 2, 2),
                      bass.AP(tensor=list_d.h,
                              offset=(e_ * 2176 + s0 + q * 128) * 2,
                              ap=[[2, 128], [1, 2]]),
                      reads=[list_d], writes=[A2[3]])
            v_copy(ITI.c(0, ntt), A2[3].sl(0, 128, 0, [[2, ntt]]),
                   reads=[A2[3]], writes=[ITI])
            for q in range(ntt):
                P.idma(A2[2].c(0, 1024),
                       None,
                       bass.AP(tensor=xn_tm.h, offset=0, ap=[[D, 128], [1, D]]),
                       bass.IndirectOffsetOnAxis(ITI.c(q, 1), 0),
                       reads=[xn_tm, ITI], writes=[A2[2]])
                for k in range(KD):
                    pst = PSR[k % 4]
                    tpose(pst, pst.c(0, 128), A2[2].c(k * 128, 128),
                          reads=[A2[2]])
                    dst = A2[0] if k < 4 else A2[1]
                    v_copy(dst.c((k % 4) * nslot + q * 128, 128), pst.c(0, 128),
                           reads=[pst], writes=[dst])
            for fo in range(KF):
                slot = fo % 2
                load_group(WR1, slot, mw1t, e_ * D, KD, fo * 128)
                pst = PS0 if slot == 0 else PS1
                for kin in range(KD):
                    srcb = A2[0] if kin < 4 else A2[1]
                    mm(pst, pst.c(0, nslot), rsl(WR1, slot, kin),
                       srcb.c((kin % 4) * nslot, nslot), kin == 0, kin == KD - 1,
                       reads=[WR1, srcb])
                gdst = A8a if fo < 16 else A8b
                act(gdst.c((fo % 16) * nslot, nslot), pst.c(0, nslot), AF.Gelu,
                    bias=CV.c(CO["moeb1"] + e_ * KF + fo, 1), reads=[pst, CV],
                    writes=[gdst])
            accs = [[(PSW0, PSW0.c(0, 512)), (PSW0, PSW0.c(512, 512))],
                    [(PSW1, PSW1.c(0, 512)), (PSW1, PSW1.c(512, 512))],
                    [(PS0, PS0.c(0, 512)), (PS1, PS1.c(0, 512))],
                    [(PS2, PS2.c(0, 512)), (PS3, PS3.c(0, 512))]]
            for kf in range(KF):
                slot = kf % 2
                P.dma("sync", WR2.c(slot * 1024, 1024),
                      bass.AP(tensor=mw2t.h, offset=(e_ * FF + kf * 128) * D,
                              ap=[[D, 128], [1, D]]),
                      reads=[mw2t], writes=[WR2])
                gsrc = A8a if kf < 16 else A8b
                for q in range(ntt):
                    for hf in range(2):
                        pt, oap = accs[q][hf]
                        mm(pt, oap, gsrc.c((kf % 16) * nslot + q * 128, 128),
                           WR2.sl(0, 128, slot * 1024 + hf * 512, [[1, 512]]),
                           kf == 0, kf == KF - 1, reads=[WR2, gsrc])
            for q in range(ntt):
                for hf in range(2):
                    pt, oap = accs[q][hf]
                    v_ts1(SM.c(hf * 512, 512), oap, A2[3].c(q * 2 + 1, 1),
                          OP.mult, reads=[pt, A2[3]], writes=[SM])
                P.idma(bass.AP(tensor=moe_d.h, offset=0, ap=[[D, 128], [1, D]]),
                       bass.IndirectOffsetOnAxis(ITI.c(q, 1), 0),
                       SM.c(0, 1024), None, accum=True,
                       reads=[SM, ITI], writes=[moe_d])

    for tt in range(T // 128):
        P.dma("sync", A2[2].c(0, 1024),
              bass.AP(tensor=moe_d.h, offset=tt * 128 * D, ap=[[D, 128], [1, D]]),
              reads=[moe_d], writes=[A2[2]])
        for k in range(KD):
            tpose(PS2, PS2.c(0, 128), A2[2].c(k * 128, 128), reads=[A2[2]])
            v_tt(X.c(k * T + tt * 128, 128), X.c(k * T + tt * 128, 128),
                 PS2.c(0, 128), OP.add, reads=[X, PS2], writes=[X])

    if DEBUG:
        dbg_xC = TB(nc.dram_tensor("dbg_xC", [D, T], F32, kind="ExternalOutput"),
                    "dbg_xC", T)
        P.dma("sync", bass.AP(tensor=dbg_xC.h, offset=0,
                              ap=[[T, 128], [128 * T, KD], [1, T]]),
              X.c(0, KD * T), reads=[X], writes=[dbg_xC])

    for c in range(NCH):
        t0 = c * TC
        ln_stats(lambda k: (X.c(k * T + t0, TC), X), TC, X)
        for k in range(KD):
            ln_apply(X.c(k * T + t0, TC), X, XNC.c(k * TC, TC), XNC, TC)
        P.dma("sync", bass.AP(tensor=xn1_d.h, offset=t0,
                              ap=[[T, 128], [128 * T, KD], [1, TC]]),
              XNC.sl(0, 128, 0, [[TC, KD], [1, TC]]), reads=[XNC], writes=[xn1_d])

    def xc_tb(k):
        return (A8a, (k % 4) * 2048) if k < 4 else (A8b, (k % 4) * 2048)

    for ts_ in range(4):
        t0 = ts_ * 512
        for kin in range(KD):
            dst = A2[kin // 4]
            P.dma("sync", dst.c((kin % 4) * 512, 512), fm_dram(xn1_d, kin, t0, 512),
                  reads=[xn1_d], writes=[dst])
        for jo in range(16):
            slot = jo % 2
            load_group(WR1, slot, w_in, 0, KD, jo * 128)
            pst = PS0 if jo % 2 == 0 else PS1
            for kin in range(KD):
                src = A2[kin // 4]
                mm(pst, pst.c(0, 512), rsl(WR1, slot, kin),
                   src.c((kin % 4) * 512, 512), kin == 0, kin == KD - 1,
                   reads=[WR1, src])
            if jo < 8:
                tbx, xoff = xc_tb(jo)
                act(tbx.c(xoff + t0, 512), pst.c(0, 512), AF.Identity,
                    bias=CV.c(CO["bin"] + jo, 1), reads=[pst, CV], writes=[tbx])
            else:
                act(SM.c(0, 512), pst.c(0, 512), AF.Silu,
                    bias=CV.c(CO["bin"] + jo, 1), reads=[pst, CV], writes=[SM])
                P.dma("sync", fm_dram(xg_d, jo - 8, t0, 512), SM.c(0, 512),
                      reads=[SM], writes=[xg_d])
    for k in range(KD):
        tbx, xoff = xc_tb(k)
        v_ts1(A2[0].c(0, T), tbx.c(xoff, T), CV.c(CO["convw"] + k * 4 + 3, 1),
              OP.mult, reads=[tbx, CV], writes=[A2[0]])
        for j in (2, 1, 0):
            sh = 3 - j
            v_stt(A2[0].c(sh, T - sh), tbx.c(xoff, T - sh),
                  CV.c(CO["convw"] + k * 4 + j, 1), A2[0].c(sh, T - sh),
                  OP.mult, OP.add, reads=[tbx, CV, A2[0]], writes=[A2[0]])
        act(tbx.c(xoff, T), A2[0].c(0, T), AF.Silu, bias=CV.c(CO["convb"] + k, 1),
            reads=[A2[0], CV], writes=[tbx])
    for ts_ in range(4):
        t0 = ts_ * 512
        for kin in range(KD):
            tbx, xoff = xc_tb(kin)
            mm(PS2, PS2.c(0, 512, 0, 32), WBCs.c(kin * 32, 32),
               tbx.c(xoff + t0, 512), kin == 0, kin == KD - 1, reads=[WBCs, tbx])
        def fnb(e, o=SM.sl(0, 32, 0, [[1, 512]]), i=PS2.sl(0, 32, 0, [[1, 512]]),
                b=CV.sl(0, 32, CO["bbc"], [[1, 1]])):
            return e.activation(out=o, in_=i, func=AF.Identity, bias=b, scale=1.0)
        P.op("scalar", fnb, reads=[PS2, CV], writes=[SM])
        P.dma("sync", bass.AP(tensor=bc_d.h, offset=t0, ap=[[T, 32], [1, 512]]),
              SM.sl(0, 32, 0, [[1, 512]]), reads=[SM], writes=[bc_d])
    for ts_ in range(4):
        t0 = ts_ * 512
        for jo in range(KD):
            slot = jo % 2
            load_group(WR1, slot, w_dt, 0, KD, jo * 128)
            pst = PS0 if jo % 2 == 0 else PS1
            for kin in range(KD):
                tbx, xoff = xc_tb(kin)
                mm(pst, pst.c(0, 512), rsl(WR1, slot, kin),
                   tbx.c(xoff + t0, 512), kin == 0, kin == KD - 1,
                   reads=[WR1, tbx])
            act(SM.c(0, 512), pst.c(0, 512), AF.Identity,
                bias=CV.c(CO["bdt"] + jo, 1), reads=[pst, CV], writes=[SM])
            act(SM.c(512, 512), SM.c(0, 512), AF.Abs, reads=[SM], writes=[SM])
            act(SM.c(512, 512), SM.c(512, 512), AF.Exp, scale=-1.0, reads=[SM],
                writes=[SM])
            act(SM.c(512, 512), SM.c(512, 512), AF.Ln, bias=1.0, reads=[SM],
                writes=[SM])
            def fnre(e, o=SM2.c(0, 512), i=SM.c(0, 512)):
                return e.tensor_relu(out=o, in_=i)
            P.op("vector", fnre, reads=[SM], writes=[SM2])
            v_tt(SM.c(0, 512), SM2.c(0, 512), SM.c(512, 512), OP.add,
                 reads=[SM, SM2], writes=[SM])
            P.dma("sync", fm_dram(dt_d, jo, t0, 512), SM.c(0, 512), reads=[SM],
                  writes=[dt_d])
    memset(MST, 0.0)
    for cs_ in range(T // LM):
        t0 = cs_ * LM
        for n in range(NST):
            P.dma("sync", A2[1].c(n * LM, LM),
                  bass.AP(tensor=bc_d.h, offset=n * T + t0, ap=[[0, 128], [1, LM]]),
                  reads=[bc_d], writes=[A2[1]])
            P.dma("sync", A2[2].c(n * LM, LM),
                  bass.AP(tensor=bc_d.h, offset=(NST + n) * T + t0,
                          ap=[[0, 128], [1, LM]]),
                  reads=[bc_d], writes=[A2[2]])
        P.dma("sync", SM.c(0, KD * LM),
              bass.AP(tensor=dt_d.h, offset=t0,
                      ap=[[T, 128], [128 * T, KD], [1, LM]]),
              reads=[dt_d], writes=[SM])
        for k in range(KD):
            tbx, xoff = xc_tb(k)
            v_tt(SM2.c(128, LM), SM.c(k * LM, LM), tbx.c(xoff + t0, LM), OP.mult,
                 reads=[SM, tbx], writes=[SM2])
            def fnp(e, o=A2[0].sl(0, 128, 0, [[LM, NST], [1, LM]]),
                    i0=SM.sl(0, 128, k * LM, [[0, NST], [1, LM]]),
                    i1=CV.sl(0, 128, CO["Aneg"] + k * NST, [[1, NST], [0, LM]])):
                return e.tensor_tensor(out=o, in0=i0, in1=i1, op=OP.mult)
            P.op("vector", fnp, reads=[SM, CV], writes=[A2[0]])
            act(A2[0].c(0, NST * LM), A2[0].c(0, NST * LM), AF.Exp,
                reads=[A2[0]], writes=[A2[0]])
            def fnu(e, o=A2[3].sl(0, 128, 0, [[LM, NST], [1, LM]]),
                    i0=SM2.sl(0, 128, 128, [[0, NST], [1, LM]]),
                    i1=A2[1].sl(0, 128, 0, [[LM, NST], [1, LM]])):
                return e.tensor_tensor(out=o, in0=i0, in1=i1, op=OP.mult)
            P.op("vector", fnu, reads=[SM2, A2[1]], writes=[A2[3]])
            for n in range(NST):
                def fns(e, o=SM2.c(256, LM), d0=A2[0].c(n * LM, LM),
                        d1=A2[3].c(n * LM, LM), ini=MST.c(k * NST + n, 1)):
                    return e.tensor_tensor_scan(out=o, data0=d0, data1=d1,
                                                initial=ini, op0=OP.mult,
                                                op1=OP.add)
                P.op("vector", fns, reads=[A2[0], A2[3], MST], writes=[SM2])
                v_copy(MST.c(k * NST + n, 1), SM2.c(256 + LM - 1, 1),
                       reads=[SM2], writes=[MST])
                v_tt(A2[3].c(n * LM, LM), SM2.c(256, LM), A2[2].c(n * LM, LM),
                     OP.mult, reads=[SM2, A2[2]], writes=[A2[3]])
            def fnr2(e, o=SM2.c(384, LM),
                     i=A2[3].sl(0, 128, 0, [[1, LM], [LM, NST]])):
                return e.tensor_reduce(out=o, in_=i, axis=AXX, op=OP.add)
            P.op("vector", fnr2, reads=[A2[3]], writes=[SM2])
            v_stt(SM2.c(384, LM), tbx.c(xoff + t0, LM), CV.c(CO["Dv"] + k, 1),
                  SM2.c(384, LM), OP.mult, OP.add, reads=[tbx, CV, SM2],
                  writes=[SM2])
            P.dma("sync", SM2.c(512, LM), fm_dram(xg_d, k, t0, LM),
                  reads=[xg_d], writes=[SM2])
            v_tt(SM2.c(640, LM), SM2.c(384, LM), SM2.c(512, LM), OP.mult,
                 reads=[SM2], writes=[SM2])
            P.dma("sync", fm_dram(ys_d, k, t0, LM), SM2.c(640, LM), reads=[SM2],
                  writes=[ys_d])
    for ts_ in range(4):
        t0 = ts_ * 512
        for kin in range(KD):
            dst = A2[kin // 4]
            P.dma("sync", dst.c((kin % 4) * 512, 512), fm_dram(ys_d, kin, t0, 512),
                  reads=[ys_d], writes=[dst])
        for jo in range(KD):
            slot = jo % 2
            load_group(WR1, slot, w_out, 0, KD, jo * 128)
            pst = PS0 if jo % 2 == 0 else PS1
            for kin in range(KD):
                src = A2[kin // 4]
                mm(pst, pst.c(0, 512), rsl(WR1, slot, kin),
                   src.c((kin % 4) * 512, 512), kin == 0, kin == KD - 1,
                   reads=[WR1, src])
            tbr, roff = xc_tb(jo)
            act(SM.c(0, 512), pst.c(0, 512), AF.Identity,
                bias=CV.c(CO["bout"] + jo, 1), reads=[pst, CV], writes=[SM])
            P.dma("sync", A2[2].c(0, 512), fm_dram(xn1_d, jo, t0, 512),
                  reads=[xn1_d], writes=[A2[2]])
            v_tt(tbr.c(roff + t0, 512), SM.c(0, 512), A2[2].c(0, 512),
                 OP.add, reads=[SM, A2[2]], writes=[tbr])
        ln_stats(lambda k: (xc_tb(k)[0].c(xc_tb(k)[1] + t0, 512), xc_tb(k)[0]),
                 512, None)
        for k in range(KD):
            tbr, roff = xc_tb(k)
            ln_apply(tbr.c(roff + t0, 512), tbr, SM.c(0, 512), SM, 512,
                     gcol=CV.c(CO["lngm"] + k, 1), bcol=CV.c(CO["lnbm"] + k, 1))
            v_tt(X.c(k * T + t0, 512), X.c(k * T + t0, 512), SM.c(0, 512),
                 OP.add, reads=[X, SM], writes=[X])

    if DEBUG:
        dbg_xE = TB(nc.dram_tensor("dbg_xE", [D, T], F32, kind="ExternalOutput"),
                    "dbg_xE", T)
        P.dma("sync", bass.AP(tensor=dbg_xE.h, offset=0,
                              ap=[[T, 128], [128 * T, KD], [1, T]]),
              X.c(0, KD * T), reads=[X], writes=[dbg_xE])

    for k in range(KD):
        P.dma("sync", A2[0].c(0, T), fm_dram(xn1_d, k, 0, T), reads=[xn1_d],
              writes=[A2[0]])
        v_red(STV.c(96 + k, 1), A2[0].c(0, T), OP.max, reads=[A2[0]],
              writes=[STV], absv=True)
    v_red(STV.c(104, 1), STV.c(96, 8), OP.max, reads=[STV], writes=[STV])
    tpose(PS2, PS2.sl(0, 1, 0, [[1, 128]]), STV.sl(0, 128, 104, [[1, 1]]),
          reads=[STV])
    v_copy(SM.c(0, 128, 0, 1), PS2.c(0, 128, 0, 1), reads=[PS2], writes=[SM])
    v_red(GSC.c(2, 1, 0, 1), SM.c(0, 128, 0, 1), OP.max, reads=[SM], writes=[GSC])
    v_copy(GSC.c(10, 4, 0, 1), GSC.sl(0, 1, 2, [[0, 4]]), reads=[GSC],
           writes=[GSC])
    P.dma("gpsimd", cc_in1.c(0, 4, 0, 1), GSC.c(10, 4, 0, 1),
          reads=[GSC], writes=[cc_in1])
    def fncc1(e, i=cc_in1.ap().opt(), o=cc_out1.ap().opt()):
        return e.collective_compute("AllReduce", OP.max,
                                    replica_groups=[list(range(NCORE))],
                                    ins=[i], outs=[o])
    P.op("gpsimd", fncc1, reads=[cc_in1], writes=[cc_out1])
    P.dma("gpsimd", GSC.c(3, 1, 0, 1), cc_out1.c(0, 1, 0, 1), reads=[cc_out1],
          writes=[GSC])
    act(GSC.c(4, 1, 0, 1), GSC.c(0, 1, 0, 1), AF.Sigmoid, reads=[GSC],
        writes=[GSC])
    v_tt(GSC.c(4, 1, 0, 1), GSC.c(4, 1, 0, 1), GSC.c(3, 1, 0, 1), OP.mult,
         reads=[GSC], writes=[GSC])
    v_ts1(GSC.c(4, 1, 0, 1), GSC.c(4, 1, 0, 1), R127, OP.mult, reads=[GSC],
          writes=[GSC])
    v_recip(GSC.c(5, 1, 0, 1), GSC.c(4, 1, 0, 1), reads=[GSC], writes=[GSC])
    P.dma("sync", SXR.c(0, 1), GSC.sl(0, 1, 5, [[0, 128], [1, 1]]), reads=[GSC],
          writes=[SXR])
    P.dma("sync", SXR.c(1, 1), GSC.sl(0, 1, 4, [[0, 128], [1, 1]]), reads=[GSC],
          writes=[SXR])
    for k in range(KD):
        tbx, xoff = xc_tb(k)
        for hh in range(2):
            P.dma("sync", A2[1].c(0, 1024), fm_dram(xn1_d, k, hh * 1024, 1024),
                  reads=[xn1_d], writes=[A2[1]])
            for qq in range(2):
                quant_tile(tbx.c(xoff + hh * 1024 + qq * 512, 512), tbx,
                           A2[1].c(qq * 512, 512), A2[1], SXR.c(0, 1), SXR, 512,
                           clip=True, scratch=A2[0])
    for ic in range(KI):
        P.dma("sync", SM.c(0, 1024),
              bass.AP(tensor=wg_raw.h, offset=ic * 128 * D, ap=[[D, 128], [1, D]]),
              reads=[wg_raw], writes=[SM])
        v_red(SM2.c(0, 1), SM.c(0, 1024), OP.max, reads=[SM], writes=[SM2],
              absv=True)
        v_ts1(SM2.c(1, 1), SM2.c(0, 1), R127, OP.mult, reads=[SM2], writes=[SM2])
        v_recip(SM2.c(2, 1), SM2.c(1, 1), reads=[SM2], writes=[SM2])
        v_tt(SM2.c(3, 1), SM2.c(1, 1), SXR.c(1, 1), OP.mult, reads=[SM2, SXR],
             writes=[SM2])
        for qq in range(2):
            quant_tile(SM.c(qq * 512, 512), SM, SM.c(qq * 512, 512), SM,
                       SM2.c(2, 1), SM2, 512, clip=True, scratch=A2[0])
        for hk in range(KD):
            tpose(PS2, PS2.c(0, 128), SM.c(hk * 128, 128), reads=[SM])
            v_copy(A2[1].c(hk * 128, 128), PS2.c(0, 128), reads=[PS2],
                   writes=[A2[1]])
        for ts_ in range(4):
            for hk in range(KD):
                tbxq, xqoff = xc_tb(hk)
                mm(PS0, PS0.c(0, 512), A2[1].c(hk * 128, 128),
                   tbxq.c(xqoff + ts_ * 512, 512), hk == 0, hk == KD - 1,
                   reads=[A2[1], tbxq])
            act(A2[2].c(ts_ * 512, 512), PS0.c(0, 512), AF.Silu,
                scale=SM2.c(3, 1), reads=[PS0, SM2], writes=[A2[2]])
        P.dma("sync", SM.c(0, 1024),
              bass.AP(tensor=wu_raw.h, offset=ic * 128 * D, ap=[[D, 128], [1, D]]),
              reads=[wu_raw], writes=[SM])
        v_red(SM2.c(4, 1), SM.c(0, 1024), OP.max, reads=[SM], writes=[SM2],
              absv=True)
        v_ts1(SM2.c(5, 1), SM2.c(4, 1), R127, OP.mult, reads=[SM2], writes=[SM2])
        v_recip(SM2.c(6, 1), SM2.c(5, 1), reads=[SM2], writes=[SM2])
        v_tt(SM2.c(7, 1), SM2.c(5, 1), SXR.c(1, 1), OP.mult, reads=[SM2, SXR],
             writes=[SM2])
        for qq in range(2):
            quant_tile(SM.c(qq * 512, 512), SM, SM.c(qq * 512, 512), SM,
                       SM2.c(6, 1), SM2, 512, clip=True, scratch=A2[0])
        for hk in range(KD):
            tpose(PS2, PS2.c(0, 128), SM.c(hk * 128, 128), reads=[SM])
            v_copy(A2[1].c(hk * 128, 128), PS2.c(0, 128), reads=[PS2],
                   writes=[A2[1]])
        for ts_ in range(4):
            for hk in range(KD):
                tbxq, xqoff = xc_tb(hk)
                mm(PS1, PS1.c(0, 512), A2[1].c(hk * 128, 128),
                   tbxq.c(xqoff + ts_ * 512, 512), hk == 0, hk == KD - 1,
                   reads=[A2[1], tbxq])
            act(A2[3].c(ts_ * 512, 512), PS1.c(0, 512), AF.Identity,
                scale=SM2.c(7, 1), reads=[PS1, SM2], writes=[A2[3]])
        v_tt(A2[2].c(0, T), A2[2].c(0, T), A2[3].c(0, T), OP.mult,
             reads=[A2[2], A2[3]], writes=[A2[2]])
        v_red(STV.c(ic, 1), A2[2].c(0, T), OP.max, reads=[A2[2]], writes=[STV],
              absv=True)
        P.dma("sync", bass.AP(tensor=h_d.h, offset=ic * 128 * T,
                              ap=[[T, 128], [1, T]]),
              A2[2].c(0, T), reads=[A2[2]], writes=[h_d])
    v_red(STV.c(105, 1), STV.c(0, KI), OP.max, reads=[STV], writes=[STV])
    tpose(PS2, PS2.sl(0, 1, 0, [[1, 128]]), STV.sl(0, 128, 105, [[1, 1]]),
          reads=[STV])
    v_copy(SM.c(0, 128, 0, 1), PS2.c(0, 128, 0, 1), reads=[PS2], writes=[SM])
    v_red(GSC.c(6, 1, 0, 1), SM.c(0, 128, 0, 1), OP.max, reads=[SM], writes=[GSC])
    v_copy(GSC.c(10, 4, 0, 1), GSC.sl(0, 1, 6, [[0, 4]]), reads=[GSC],
           writes=[GSC])
    P.dma("gpsimd", cc_in2.c(0, 4, 0, 1), GSC.c(10, 4, 0, 1),
          reads=[GSC], writes=[cc_in2])
    def fncc2(e, i=cc_in2.ap().opt(), o=cc_out2.ap().opt()):
        return e.collective_compute("AllReduce", OP.max,
                                    replica_groups=[list(range(NCORE))],
                                    ins=[i], outs=[o])
    P.op("gpsimd", fncc2, reads=[cc_in2], writes=[cc_out2])
    P.dma("gpsimd", GSC.c(7, 1, 0, 1), cc_out2.c(0, 1, 0, 1), reads=[cc_out2],
          writes=[GSC])
    act(GSC.c(8, 1, 0, 1), GSC.c(1, 1, 0, 1), AF.Sigmoid, reads=[GSC],
        writes=[GSC])
    v_tt(GSC.c(8, 1, 0, 1), GSC.c(8, 1, 0, 1), GSC.c(7, 1, 0, 1), OP.mult,
         reads=[GSC], writes=[GSC])
    v_ts1(GSC.c(8, 1, 0, 1), GSC.c(8, 1, 0, 1), R127, OP.mult, reads=[GSC],
          writes=[GSC])
    v_recip(GSC.c(9, 1, 0, 1), GSC.c(8, 1, 0, 1), reads=[GSC], writes=[GSC])
    P.dma("sync", SXR.c(2, 1), GSC.sl(0, 1, 9, [[0, 128], [1, 1]]), reads=[GSC],
          writes=[SXR])
    P.dma("sync", SXR.c(3, 1), GSC.sl(0, 1, 8, [[0, 128], [1, 1]]), reads=[GSC],
          writes=[SXR])
    for ic in range(KI):
        for hh in range(2):
            P.dma("sync", A2[1].c(0, 1024),
                  bass.AP(tensor=h_d.h, offset=ic * 128 * T + hh * 1024,
                          ap=[[T, 128], [1, 1024]]),
                  reads=[h_d], writes=[A2[1]])
            for qq in range(2):
                quant_tile(A2[1].c(qq * 512, 512), A2[1], A2[1].c(qq * 512, 512),
                           A2[1], SXR.c(2, 1), SXR, 512, clip=True,
                           scratch=A2[0])
            P.dma("sync", bass.AP(tensor=hq_d.h, offset=ic * 128 * T + hh * 1024,
                                  ap=[[T, 128], [1, 1024]]),
                  A2[1].c(0, 1024), reads=[A2[1]], writes=[hq_d])
    for dch in range(KD):
        P.dma("sync", A8a.c(0, FI),
              bass.AP(tensor=wd_raw.h, offset=dch * 128 * FI,
                      ap=[[FI, 128], [1, FI]]),
              reads=[wd_raw], writes=[A8a])
        v_red(SM2.c(0, 1), A8a.c(0, FI), OP.max, reads=[A8a], writes=[SM2],
              absv=True)
        v_ts1(SM2.c(1, 1), SM2.c(0, 1), R127, OP.mult, reads=[SM2], writes=[SM2])
        v_recip(SM2.c(2, 1), SM2.c(1, 1), reads=[SM2], writes=[SM2])
        v_tt(SM2.c(3, 1), SM2.c(1, 1), SXR.c(3, 1), OP.mult, reads=[SM2, SXR],
             writes=[SM2])
        for qq in range((FI + 511) // 512):
            w = min(512, FI - qq * 512)
            quant_tile(A8a.c(qq * 512, w), A8a, A8a.c(qq * 512, w), A8a,
                       SM2.c(2, 1), SM2, w, clip=True, scratch=A2[0])
        for ik in range(KI):
            tpose(PS2, PS2.c(0, 128), A8a.c(ik * 128, 128), reads=[A8a])
            v_copy(A8a.c(4096 + ik * 128, 128), PS2.c(0, 128), reads=[PS2],
                   writes=[A8a])
        for ts_ in range(4):
            t0 = ts_ * 512
            for ik in range(KI):
                P.dma("sync", A2[1].c(0, 512),
                      bass.AP(tensor=hq_d.h, offset=ik * 128 * T + t0,
                              ap=[[T, 128], [1, 512]]),
                      reads=[hq_d], writes=[A2[1]])
                mm(PS0, PS0.c(0, 512), A8a.c(4096 + ik * 128, 128),
                   A2[1].c(0, 512), ik == 0, ik == KI - 1, reads=[A8a, A2[1]])
            act(SM.c(0, 512), PS0.c(0, 512), AF.Identity, scale=SM2.c(3, 1),
                reads=[PS0, SM2], writes=[SM])
            v_tt(X.c(dch * T + t0, 512), X.c(dch * T + t0, 512), SM.c(0, 512),
                 OP.add, reads=[X, SM], writes=[X])

    if DEBUG:
        for nm, src, shp in [("dbg_xn0", xn0_d, (D, T)), ("dbg_moe", moe_d, (T, D)),
                             ("dbg_xn1", xn1_d, (D, T)), ("dbg_xg", xg_d, (D, T)),
                             ("dbg_dt", dt_d, (D, T)), ("dbg_bc", bc_d, (32, T)),
                             ("dbg_ys", ys_d, (D, T)), ("dbg_h", h_d, (FI, T)),
                             ("dbg_hq", hq_d, (FI, T))]:
            dtb = TB(nc.dram_tensor(nm, list(shp), F32, kind="ExternalOutput"),
                     nm, shp[1])
            P.dma("sync", dtb.ap(), src.ap(), reads=[src], writes=[dtb])
        dgs = TB(nc.dram_tensor("dbg_gsc", [1, 16], F32, kind="ExternalOutput"),
                 "dbg_gsc", 16)
        P.dma("sync", dgs.c(0, 16, 0, 1), GSC.c(0, 16, 0, 1), reads=[GSC],
              writes=[dgs])
        dlg = TB(nc.dram_tensor("dbg_lg", [128, 128], F32, kind="ExternalOutput"),
                 "dbg_lg", 128)
        P.dma("sync", dlg.ap(), LG.c(0, 128), reads=[LG], writes=[dlg])
        dme = TB(nc.dram_tensor("dbg_me", [128, 128], F32, kind="ExternalOutput"),
                 "dbg_me", 128)
        P.dma("sync", dme.ap(), ME.c(0, 128), reads=[ME], writes=[dme])

    P.dma("sync", bass.AP(tensor=y_out.h, offset=0,
                          ap=[[T, 128], [128 * T, KD], [1, T]]),
          X.c(0, KD * T), reads=[X], writes=[y_out])
    P.barrier_wait("sync", [y_out])

    P.finalize()
    return P.nc


def _prep_host(inputs):
    f32 = np.float32

    def cnp(a):
        return np.ascontiguousarray(np.asarray(a, dtype=f32))

    ins = {}
    ins["wrt"] = cnp(inputs["L0_attn_Wr"].T)
    ins["wkt"] = cnp(inputs["L0_attn_Wk"].T)
    ins["wvt"] = cnp(inputs["L0_attn_Wv"].T)
    ins["wot"] = cnp(inputs["L0_attn_Wo"].T)
    ins["w1t"] = cnp(inputs["L0_attn_W1"].T)
    ins["w2t"] = cnp(inputs["L0_attn_W2"].T)
    ins["mw1t"] = cnp(np.transpose(inputs["L0_ffn_w1"], (0, 2, 1)).reshape(NE * D, FF))
    ins["mw2t"] = cnp(np.transpose(inputs["L0_ffn_w2"], (0, 2, 1)).reshape(NE * FF, D))
    ins["b2moe"] = cnp(inputs["L0_ffn_b2"])
    ins["w_in"] = cnp(inputs["L1_attn_W_in"])
    ins["w_dt"] = cnp(inputs["L1_attn_W_dt"])
    ins["w_out"] = cnp(inputs["L1_attn_W_out"])
    ins["wg_raw"] = cnp(inputs["L1_ffn_w_gate"])
    ins["wu_raw"] = cnp(inputs["L1_ffn_w_up"])
    ins["wd_raw"] = cnp(inputs["L1_ffn_w_down"])
    ins["clips"] = cnp(np.stack([inputs["L1_ffn_clip_x"][0],
                                 inputs["L1_ffn_clip_down"][0]])[None, :])

    def fmv(a):
        a = np.asarray(a, dtype=f32)
        n = a.shape[0] // 128
        return a.reshape(n, 128).T

    cv = np.zeros((128, CV_W), dtype=f32)
    cv[:, CO["ones"]] = 1.0
    cv[:, CO["dec"]:CO["dec"] + TC] = 0.9
    cv[:, CO["ln1g"]:CO["ln1g"] + KD] = fmv(inputs["L0_attn_ln1_g"])
    cv[:, CO["ln1b"]:CO["ln1b"] + KD] = fmv(inputs["L0_attn_ln1_b"])
    cv[:, CO["ln2g"]:CO["ln2g"] + KD] = fmv(inputs["L0_attn_ln2_g"])
    cv[:, CO["ln2b"]:CO["ln2b"] + KD] = fmv(inputs["L0_attn_ln2_b"])
    cv[:, CO["b1f"]:CO["b1f"] + KF] = fmv(inputs["L0_attn_b1"])
    cv[:, CO["b2f"]:CO["b2f"] + KD] = fmv(inputs["L0_attn_b2"])
    mb1 = np.asarray(inputs["L0_ffn_b1"], dtype=f32)
    cv[:, CO["moeb1"]:CO["moeb1"] + NE * KF] = np.concatenate(
        [fmv(mb1[e]) for e in range(NE)], axis=1)
    cv[:, CO["bin"]:CO["bin"] + 16] = fmv(inputs["L1_attn_b_in"])
    cw = np.asarray(inputs["L1_attn_conv_w"], dtype=f32).reshape(D, 4)
    cv[:, CO["convw"]:CO["convw"] + KD * 4] = cw.reshape(KD, 128, 4).transpose(
        1, 0, 2).reshape(128, KD * 4)
    cv[:, CO["convb"]:CO["convb"] + KD] = fmv(inputs["L1_attn_conv_b"])
    cv[:, CO["bdt"]:CO["bdt"] + KD] = fmv(inputs["L1_attn_b_dt"])
    al = np.asarray(inputs["L1_attn_A_log"], dtype=f32)
    cv[:, CO["Aneg"]:CO["Aneg"] + KD * NST] = al.reshape(KD, 128, NST).transpose(
        1, 0, 2).reshape(128, KD * NST)
    cv[:, CO["Dv"]:CO["Dv"] + KD] = fmv(inputs["L1_attn_D"])
    cv[:, CO["bout"]:CO["bout"] + KD] = fmv(inputs["L1_attn_b_out"])
    cv[:, CO["lngm"]:CO["lngm"] + KD] = fmv(inputs["L1_attn_ln_g"])
    cv[:, CO["lnbm"]:CO["lnbm"] + KD] = fmv(inputs["L1_attn_ln_b"])
    wbc = np.asarray(inputs["L1_attn_W_bc"], dtype=f32)
    cv[:, CO["wbc"]:CO["wbc"] + KD * 32] = wbc.reshape(KD, 128, 32).transpose(
        1, 0, 2).reshape(128, KD * 32)
    rw = np.asarray(inputs["L0_ffn_router_w"], dtype=f32)
    rwt = rw.T
    cv[:, CO["rwt"]:CO["rwt"] + KD * NE] = rwt.reshape(KD, 128, NE).transpose(
        1, 0, 2).reshape(128, KD * NE)
    bbc = np.asarray(inputs["L1_attn_b_bc"], dtype=f32)
    cv[:32, CO["bbc"]] = bbc
    ins["cvec"] = cnp(cv)
    return ins


def kernel(**inputs):
    shared = _prep_host(inputs)
    x = np.asarray(inputs["L0_attn_x"], dtype=np.float32)
    in_maps = []
    for c in range(NCORE):
        m = dict(shared)
        m["x_fm"] = np.ascontiguousarray(x[c].T)
        in_maps.append(m)
    nc = build_program()
    res = run_bass_kernel_spmd(nc, in_maps, core_ids=list(range(NCORE)))
    out = np.stack([np.asarray(res.results[c]["y_fm"]).T for c in range(NCORE)])
    return out.astype(np.float32)
